# version 08
# baseline (speedup 1.0000x reference)
import math
import contextlib
import numpy as np
import ml_dtypes
import concourse.bass as bass
import concourse.mybir as mybir
from concourse.ap import AP

F32 = mybir.dt.float32
BF16 = mybir.dt.bfloat16
OP = mybir.AluOpType
AF = mybir.ActivationFunctionType
AX = mybir.AxisListType
f32 = np.float32
bf16 = ml_dtypes.bfloat16


def c32(v):
    return float(np.float32(v))


D = 1024
E = 128
H = 2816
NCF = D // 128
NCH = H // 128

ERF_SCALE = f32(math.sqrt(2.0 / math.pi))
ERF_BIAS = f32(-math.sqrt(0.5) * math.sqrt(2.0 / math.pi))
BIG = f32(1e30)
QTHR = [0.0625, 0.5625, 1.5625, 3.0625, 6.25, 12.25, 25.0]
NEG_MASK = -1e30


def bucket_delta(S, NUM_BUCKETS=32, MAX_DISTANCE=128):
    n = np.arange(S)
    max_exact = NUM_BUCKETS // 2
    is_small = n < max_exact
    nf = np.maximum(n, 1).astype(np.float32)
    vl = max_exact + (np.log(nf / max_exact) / math.log(MAX_DISTANCE / max_exact)
                      * (NUM_BUCKETS - max_exact)).astype(np.int32)
    vl = np.minimum(vl, NUM_BUCKETS - 1)
    return np.where(is_small, n, vl).astype(np.int64)


def build_consts(S):
    ident = np.eye(128, dtype=np.float32).astype(bf16)
    W = 512 + S
    s33 = np.zeros((33, W), dtype=np.float32)
    s33[32, :512] = NEG_MASK
    bd = bucket_delta(S)
    s33[bd, np.arange(512, 512 + S)] = np.float32(128.0 ** 0.5)
    return ident, s33, W


class Rsrc:
    __slots__ = ("w_q", "w_seq", "w_vc", "readers")

    def __init__(self):
        self.w_q = None
        self.w_seq = 0
        self.w_vc = None
        self.readers = {}


class Ser:

    class Q:
        __slots__ = ("name", "stream", "sem", "count", "dma", "pending",
                     "subs", "sub_uses", "K")

        def __init__(q, name, stream, sem, dma, subs=None):
            q.name = name
            q.stream = stream
            q.sem = sem
            q.count = 0
            q.dma = dma
            q.pending = False
            q.subs = subs or []
            q.sub_uses = [0] * len(q.subs)
            q.K = len(q.subs)

        def wait_spec(q, k):
            j = (k - 1) % q.K
            return q.subs[j], 16 * ((k - 1) // q.K + 1)

    PARANOID = False
    PARA_SKIP = None
    DUMP = None
    NO_COVER = False
    WQ_SINGLES = False
    WQ_FRONT = False
    WQ_NOAV = False
    WQ_STAGE = 4

    def __init__(self, nc):
        self.nc = nc
        self.res = {}
        self.wm = {}
        self.idx = 0
        self.fence_counts = {}
        mkq = lambda name, stream, dma: Ser.Q(name, stream, nc.alloc_semaphore(name="sem_" + name), dma)

        def mkdq(name, stream, K):
            subs = [nc.alloc_semaphore(name="sem_%s%d" % (name, j)) for j in range(K)]
            return Ser.Q(name, stream, None, True, subs=subs)

        self.T = mkq("T", nc.tensor, False)
        self.V = mkq("V", nc.vector, False)
        self.S = mkq("S", nc.scalar, False)
        self.G = mkq("G", nc.gpsimd, False)
        self.DSY = mkdq("DSY", nc.sync, 8)
        self.DACT = mkdq("DACT", nc.scalar, 2)
        self.DGP = mkdq("DGP", nc.gpsimd, 2)
        self.queues = [self.T, self.V, self.S, self.G, self.DSY, self.DACT,
                       self.DGP]
        self.fq = {q: mkq("F" + q.name, q.stream, False)
                   for q in (self.DSY, self.DACT, self.DGP)}

    def _r(self, key):
        r = self.res.get(key)
        if r is None:
            r = self.res[key] = Rsrc()
        return r

    def do(self, q, fn, rd=(), wr=(), sync=True):
        deps = {}

        def add(dq, ds, dvc):
            cur = deps.get(dq)
            if cur is None or cur[0] < ds:
                deps[dq] = (ds, dvc)

        for k in rd:
            r = self.res.get(k)
            if r is not None and r.w_q is not None:
                add(r.w_q, r.w_seq, r.w_vc)
        for k in wr:
            r = self.res.get(k)
            if r is None:
                continue
            if r.w_q is not None:
                add(r.w_q, r.w_seq, r.w_vc)
            for dq, (ds, dvc) in r.readers.items():
                add(dq, ds, dvc)
        for dq, ds in self.fence_counts.items():
            add(dq, ds, None)
        paranoid = Ser.PARANOID
        if Ser.PARA_SKIP is not None:
            lo, hi = Ser.PARA_SKIP
            paranoid = not (lo <= self.idx < hi)
        self.idx += 1
        if paranoid:
            for dq in self.queues:
                if dq is not q and dq.count and not dq.pending:
                    add(dq, dq.count, None)
        def covers(dq, claim, need):
            if dq.dma:
                return claim == need or claim >= need + dq.K
            return claim >= need

        items = list(deps.items())
        kept = {}
        for dq, (ds, dvc) in items:
            covered = False
            if not Ser.NO_COVER:
                for dq2, (ds2, dvc2) in items:
                    if dq2 is dq or dvc2 is None:
                        continue
                    if covers(dq, dvc2.get(dq, 0), ds):
                        covered = True
                        break
            if not covered:
                kept[dq] = (ds, dvc)
        need_drain = False

        def emit_wait(dq, ds):
            wmk = (q.stream, dq)
            if dq.dma:
                prev = self.wm.get(wmk, 0)
                if not (prev and covers(dq, prev, ds)):
                    sem, thr = dq.wait_spec(ds)
                    q.stream.wait_ge(sem, thr)
                    if ds > prev:
                        self.wm[wmk] = ds
            else:
                if self.wm.get(wmk, 0) < ds:
                    q.stream.wait_ge(dq.sem, ds)
                    self.wm[wmk] = ds

        for dq, (ds, dvc) in kept.items():
            if dq is q:
                if q.dma:
                    emit_wait(dq, ds)
                elif q is self.V and ds >= q.count:
                    need_drain = True
                continue
            if dq.pending and ds > dq.count:
                raise RuntimeError(f"cross-queue dep on pending {dq.name}")
            emit_wait(dq, ds)
        if Ser.DUMP and Ser.DUMP[0] <= self.idx - 1 < Ser.DUMP[1]:
            import sys
            print("[dump]", self.idx - 1, q.name, "rd", list(rd), "wr", list(wr),
                  "kept", {dq.name: ds for dq, (ds, _) in kept.items()},
                  "alldeps", {dq.name: ds for dq, (ds, _) in deps.items()},
                  "drain", need_drain, file=sys.stderr)
        if need_drain:
            q.stream.drain()
        if q.dma:
            jj = q.count % q.K
            if q.sub_uses[jj]:
                q.stream.wait_ge(q.subs[jj], 16 * q.sub_uses[jj])
        r = fn()
        if sync:
            if q.dma:
                jj = q.count % q.K
                r.then_inc(q.subs[jj], 16)
                q.sub_uses[jj] += 1
                q.count += 1
            else:
                r.then_inc(q.sem, 1)
                q.count += 1
            q.pending = False
            seq = q.count
        else:
            q.pending = True
            seq = q.count + 1
        myvc = {q: seq}
        for dq, (ds, dvc) in deps.items():
            if dq is not q and myvc.get(dq, 0) < ds:
                myvc[dq] = ds
            if dvc is not None:
                for q3, s3 in dvc.items():
                    if myvc.get(q3, 0) < s3:
                        myvc[q3] = s3
        for k in wr:
            rs = self._r(k)
            rs.w_q = q
            rs.w_seq = seq
            rs.w_vc = myvc
            rs.readers = {}
        for k in rd:
            rs = self._r(k)
            rs.readers[q] = (seq, myvc)
        return r

    def fence(self):
        for q in self.queues:
            if q.dma:
                if not q.count:
                    continue
                for j in range(q.K):
                    if q.sub_uses[j]:
                        q.stream.wait_ge(q.subs[j], 16 * q.sub_uses[j])
                fq = self.fq[q]
                q.stream.sem_inc(fq.sem, 1)
                fq.count += 1
                self.fence_counts[fq] = fq.count
            elif q.count:
                self.fence_counts[q] = q.count

    def dma(self, q, out, in_, rd=(), wr=()):
        return self.do(q, lambda: q.stream.dma_start(out, in_), rd=rd, wr=wr)

    def finish(self):
        for q in self.queues:
            if not q.count:
                continue
            if q.dma:
                for j in range(q.K):
                    if q.sub_uses[j]:
                        self.nc.sync.wait_ge(q.subs[j], 16 * q.sub_uses[j])
            else:
                self.nc.sync.wait_ge(q.sem, q.count)


class Plan:

    def __init__(self):
        self.ops = []

    def do(self, q, fn, rd=(), wr=(), sync=True):
        self.ops.append((q, fn, rd, wr, sync))

    def dma(self, q, out, in_, rd=(), wr=()):
        self.ops.append((q, lambda q=q, out=out, in_=in_: q.stream.dma_start(out, in_),
                         rd, wr, True))


def emit(ser, *plans):
    its = [list(p.ops) for p in plans]
    pos = [0] * len(its)
    left = sum(len(x) for x in its)
    while left:
        for j, ops in enumerate(its):
            if pos[j] < len(ops):
                q, fn, rd, wr, sync = ops[pos[j]]
                ser.do(q, fn, rd=rd, wr=wr, sync=sync)
                pos[j] += 1
                left -= 1


def build(nc, S=2048, NSEQ=2, taps=()):
    NT = S // 128
    NQST = S // 512
    NTT = NSEQ * NT
    GSZ = 4 if NTT % 4 == 0 else NTT
    NGRP = NTT // GSZ
    NBT = NT + 3
    ident_np, s33_np, W = build_consts(S)

    x_d = nc.dram_tensor("x", [NSEQ, S, D], F32, kind="ExternalInput").ap()
    wqk_d = nc.dram_tensor("wqk", [D, E], F32, kind="ExternalInput").ap()
    bqk_d = nc.dram_tensor("bqk", [E, 1], F32, kind="ExternalInput").ap()
    gam_d = nc.dram_tensor("gam", [2, E], F32, kind="ExternalInput").ap()
    bet_d = nc.dram_tensor("bet", [2, E], F32, kind="ExternalInput").ap()
    wv_d = nc.dram_tensor("wv", [D, D], F32, kind="ExternalInput").ap()
    bv_d = nc.dram_tensor("bv", [1, D], F32, kind="ExternalInput").ap()
    rel_d = nc.dram_tensor("rel", [32, 1], F32, kind="ExternalInput").ap()
    wg_d = nc.dram_tensor("wg", [H, D], F32, kind="ExternalInput").ap()
    wu_d = nc.dram_tensor("wu", [H, D], F32, kind="ExternalInput").ap()
    wd_d = nc.dram_tensor("wd", [D, H], F32, kind="ExternalInput").ap()
    out_d = nc.dram_tensor("out", [NSEQ, S, D], F32, kind="ExternalOutput").ap()

    ext2_d = nc.dram_tensor("ext2", [128, W], F32, kind="Internal").ap()
    x2s_d = nc.dram_tensor("x2s", [NSEQ, S, D], F32, kind="Internal").ap()
    wgqT_d = nc.dram_tensor("wgqT", [D, H], BF16, kind="Internal").ap()
    wuqT_d = nc.dram_tensor("wuqT", [D, H], BF16, kind="Internal").ap()
    wdqT_d = nc.dram_tensor("wdqT", [H, D], BF16, kind="Internal").ap()

    tap_d = {}
    for name, shape, dt in taps:
        tap_d[name] = nc.dram_tensor("tap_" + name, shape, dt,
                                     kind="ExternalOutput").ap()

    ident_d = nc.inline_tensor(ident_np, name="ident").ap()
    s33_d = nc.inline_tensor(s33_np, name="s33").ap()

    ser = Ser(nc)
    T, V, Sc, G = ser.T, ser.V, ser.S, ser.G
    DSY, DACT, DGP = ser.DSY, ser.DACT, ser.DGP

    stack = contextlib.ExitStack()
    PA = stack.enter_context(nc.psum_tensor("PA", [128, 512], F32)).ap()
    PB = stack.enter_context(nc.psum_tensor("PB", [128, 512], F32)).ap()
    PC = stack.enter_context(nc.psum_tensor("PC", [128, 512], F32)).ap()
    PD = stack.enter_context(nc.psum_tensor("PD", [128, 512], F32)).ap()
    PT0 = stack.enter_context(nc.psum_tensor("PT0", [128, 1024], BF16)).ap()
    PT1 = stack.enter_context(nc.psum_tensor("PT1", [128, 1024], BF16)).ap()
    PTs = [PT0, PT1]
    BKS = [(PA, "PA", PB, "PB"), (PC, "PC", PD, "PD")]

    def sb(name, shape, dt):
        return stack.enter_context(nc.sbuf_tensor(name, shape, dt)).ap()

    ident_s = sb("ident_s", [128, 128], BF16)
    bqk_s = sb("bqk_s", [128, 1], F32)
    g0s = sb("g0s", [128, 1], F32)
    b0s = sb("b0s", [128, 1], F32)
    g1s = sb("g1s", [128, 1], F32)
    b1s = sb("b1s", [128, 1], F32)
    gb_tmp = sb("gb_tmp", [128, 4], F32)
    strip = sb("strip", [128, 6 * NT * NSEQ], F32)

    def strip_col(s, which, tt):
        return (s * 6 + which) * NT + tt

    ser.dma(DSY, ident_s[:], ident_d[:], wr=["params"])
    ser.dma(DSY, bqk_s[:], bqk_d[:], wr=["params"])
    ser.dma(DSY, gb_tmp[:, 0:1], gam_d[0].rearrange("(n o) -> n o", o=1), wr=["params"])
    ser.dma(DSY, gb_tmp[:, 1:2], bet_d[0].rearrange("(n o) -> n o", o=1), wr=["params"])
    ser.dma(DSY, gb_tmp[:, 2:3], gam_d[1].rearrange("(n o) -> n o", o=1), wr=["params"])
    ser.dma(DSY, gb_tmp[:, 3:4], bet_d[1].rearrange("(n o) -> n o", o=1), wr=["params"])
    inv_s = c32(1.0 / S)
    ser.do(V, lambda: V.stream.tensor_scalar(g0s[:], gb_tmp[:, 0:1], inv_s, None, OP.mult), rd=["params"], wr=["params2"])
    ser.do(V, lambda: V.stream.tensor_scalar(b0s[:], gb_tmp[:, 1:2], inv_s, None, OP.mult), rd=["params"], wr=["params2"])
    ser.do(V, lambda: V.stream.tensor_copy(g1s[:], gb_tmp[:, 2:3]), rd=["params"], wr=["params2"])
    ser.do(V, lambda: V.stream.tensor_copy(b1s[:], gb_tmp[:, 3:4]), rd=["params"], wr=["params2"])

    def quant_tile(em, x_key, x_ap, out_key, out_ap, F, scr, pre):
        NB = F // 16
        bm, s_, rc, rb2 = scr["bm"], scr["s_"], scr["rc"], scr["rb2"]
        x2 = scr["x2"]
        z = x2
        c0, c1, c2 = scr["c0"], scr["c1"], scr["c2"]
        kbm, ks, krc, krb = pre + "bm", pre + "s_", pre + "rc", pre + "rb2"
        kx2 = pre + "x2"
        kz = kx2
        kc0, kc1, kc2 = pre + "c0", pre + "c1", pre + "c2"
        x3 = x_ap.rearrange("p (b k) -> p b k", k=16)
        em.do(Sc, lambda: Sc.stream.activation(x2[:, :F], x_ap, AF.Square),
              rd=[x_key], wr=[kx2])
        em.do(V, lambda: V.stream.tensor_reduce(bm[:, :NB], x3, AX.X, OP.max,
                                                apply_absolute_value=True),
              rd=[x_key], wr=[kbm])
        em.do(V, lambda: V.stream.reciprocal(rc[:, :NB], bm[:, :NB]),
              rd=[kbm], wr=[krc])
        em.do(V, lambda: V.stream.tensor_scalar(rc[:, :NB], rc[:, :NB], 6.0, c32(1e12), OP.mult, OP.min),
              rd=[krc], wr=[krc])
        em.do(V, lambda: V.stream.tensor_mul(rb2[:, :NB], rc[:, :NB], rc[:, :NB]),
              rd=[krc], wr=[krb])
        em.do(V, lambda: V.stream.reciprocal(s_[:, :NB], rc[:, :NB]),
              rd=[krc], wr=[ks])
        rb2b = rb2[:, :NB].rearrange("p (b o) -> p b o", o=1).broadcast_to([128, NB, 16])
        em.do(V, lambda: V.stream.tensor_tensor(
            z[:, :F].rearrange("p (b k) -> p b k", k=16),
            x2[:, :F].rearrange("p (b k) -> p b k", k=16), rb2b, OP.mult),
            rd=[kx2, krb], wr=[kz])
        zf = z[:, :F]

        def step(dst, dk, thr):
            em.do(Sc, lambda: Sc.stream.activation(
                dst, zf, AF.Sigmoid, bias=np.float32(-thr * 1e30), scale=BIG),
                rd=[kz], wr=[dk])

        c0f, c1f, c2f = c0[:, :F], c1[:, :F], c2[:, :F]
        csg = scr["sg"][:, :F]
        ksg = pre + "sg"
        em.do(Sc, lambda: Sc.stream.activation(csg, x_ap, AF.Sign),
              rd=[x_key], wr=[ksg])
        step(c0f, kc0, QTHR[0])
        step(c1f, kc1, QTHR[1])
        em.do(V, lambda: V.stream.tensor_add(c0f, c0f, c1f), rd=[kc0, kc1], wr=[kc0])
        step(c1f, kc1, QTHR[2])
        step(c2f, kc2, QTHR[3])
        em.do(V, lambda: V.stream.tensor_add(c1f, c1f, c2f), rd=[kc1, kc2], wr=[kc1])
        em.do(V, lambda: V.stream.tensor_add(c0f, c0f, c1f), rd=[kc0, kc1], wr=[kc0])
        step(c1f, kc1, QTHR[4])
        step(c2f, kc2, QTHR[5])
        em.do(V, lambda: V.stream.tensor_add(c1f, c1f, c2f), rd=[kc1, kc2], wr=[kc1])
        step(c2f, kc2, QTHR[6])
        em.do(V, lambda: V.stream.scalar_tensor_tensor(
            c1f, c2f, 2.0, c1f, OP.mult, OP.add), rd=[kc1, kc2], wr=[kc1])
        em.do(V, lambda: V.stream.scalar_tensor_tensor(
            c0f, c1f, 2.0, c0f, OP.mult, OP.add), rd=[kc0, kc1], wr=[kc0])
        em.do(V, lambda: V.stream.tensor_mul(c2f, c0f, csg), rd=[kc0, ksg], wr=[kc2])
        s_b = s_[:, :NB].rearrange("p (b o) -> p b o", o=1).broadcast_to([128, NB, 16])
        em.do(V, lambda: V.stream.scalar_tensor_tensor(
            out_ap.rearrange("p (b k) -> p b k", k=16),
            c2f.rearrange("p (b k) -> p b k", k=16), 0.5, s_b,
            OP.mult, OP.mult), rd=[kc2, ks], wr=[out_key])

    with contextlib.ExitStack() as ph:
        def psb(name, shape, dt):
            return ph.enter_context(nc.sbuf_tensor(name, shape, dt)).ap()

        rel_s = psb("rel_s", [33, 128], F32)
        s33_s = psb("s33_s", [33, W], F32)
        ext_rep = psb("ext_rep", [128, W], F32)
        rel_col = psb("rel_col", [32, 1], F32)
        ser.dma(DSY, rel_col[:], rel_d[:], wr=["rel"])
        ser.dma(DSY, s33_s[:], s33_d[:], wr=["s33"])
        ser.do(V, lambda: V.stream.tensor_copy(rel_s[0:32, :],
                                               rel_col.broadcast_to([32, 128])),
               rd=["rel"], wr=["rel_s"])
        ser.do(V, lambda: V.stream.memset(rel_s[32:33, :], 1.0), wr=["rel_s"])
        nch = (W + 511) // 512
        for c in range(nch):
            cw = min(512, W - c * 512)
            ser.do(T, lambda c=c, cw=cw: T.stream.matmul(
                PA[:, 0:cw], rel_s[:, :], s33_s[:, c * 512:c * 512 + cw],
                start=True, stop=True), rd=["rel_s", "s33"], wr=["PA"])
            ser.do(V, lambda c=c, cw=cw: V.stream.tensor_copy(
                ext_rep[:, c * 512:c * 512 + cw], PA[:, 0:cw]),
                rd=["PA"], wr=["ext_rep"])
        ser.dma(DSY, ext2_d[:], ext_rep[:], rd=["ext_rep"], wr=["d:ext2"])
    ser.fence()

    with contextlib.ExitStack() as ph:
        def psb(name, shape, dt):
            return ph.enter_context(nc.sbuf_tensor(name, shape, dt)).ap()

        q_s = psb("q_s", [128, S], BF16)
        k_s = psb("k_s", [128, S], BF16)
        biasT = psb("biasT", [128, NBT * 128], F32)
        for ot in range(NBT):
            bsrc = AP(ext2_d.tensor, 128 + ot * 128, [[W - 1, 128], [1, 128]])
            ser.dma(DSY, biasT[:, ot * 128:(ot + 1) * 128], bsrc,
                    rd=["d:ext2"], wr=["biasT"])

        wqk_s = psb("wqk_s", [128, NCF * E], BF16)
        wv_s = psb("wv_s", [128, NCF * D], BF16)
        bv_rep = psb("bv_rep", [128, D], F32)
        ser.dma(DGP, wqk_s.rearrange("p (c e) -> p c e", c=NCF),
                wqk_d.rearrange("(c p) e -> p c e", p=128), wr=["wqk_s"])
        ser.dma(DGP, wv_s.rearrange("p (c n) -> p c n", c=NCF),
                wv_d.rearrange("(c p) n -> p c n", p=128), wr=["wv_s"])
        bv_row = psb("bv_row", [1, D], F32)
        ones_r = psb("ones_r", [1, 128], F32)
        ser.dma(DSY, bv_row[:], bv_d[:], wr=["bv_row"])
        ser.do(V, lambda: V.stream.memset(ones_r[:], 1.0), wr=["ones_r"])
        for hh in range(2):
            ser.do(T, lambda hh=hh: T.stream.matmul(
                PA[:, :], ones_r[:, :], bv_row[:, hh * 512:(hh + 1) * 512],
                start=True, stop=True), rd=["ones_r", "bv_row"], wr=["PA"])
            ser.do(V, lambda hh=hh: V.stream.tensor_copy(
                bv_rep[:, hh * 512:(hh + 1) * 512], PA[:, :]),
                rd=["PA"], wr=["bv_rep"])

        WQWS = [1024, 1024]
        wt_w, wq_w, wtq_w, scrw = [], [], [], []
        for b, WQW in enumerate(WQWS):
            wt_w.append(psb("wt_w%d" % b, [128, WQW], F32))
            wq_w.append(psb("wq_w%d" % b, [128, WQW], BF16))
            wtq_w.append(psb("wtq_w%d" % b, [128, WQW], BF16))
            scrw.append({"bm": psb("w%d_bm" % b, [128, WQW // 16], F32),
                         "s_": psb("w%d_s_" % b, [128, WQW // 16], F32),
                         "rc": psb("w%d_rc" % b, [128, WQW // 16], F32),
                         "rb2": psb("w%d_rb2" % b, [128, WQW // 16], F32),
                         "x2": psb("w%d_x2" % b, [128, WQW], F32),
                         "c0": psb("w%d_c0" % b, [128, WQW], BF16),
                         "c1": psb("w%d_c1" % b, [128, WQW], BF16),
                         "c2": psb("w%d_c2" % b, [128, WQW], BF16),
                         "sg": psb("w%d_sg" % b, [128, WQW], BF16)})

        def w_jobs():
            gu = [(w_in, w_out, wok, t, 0, D)
                  for w_in, w_out, wok in ((wg_d, wgqT_d, "d:wgqT"),
                                           (wu_d, wuqT_d, "d:wuqT"))
                  for t in range(NCH)]
            wd = [(wd_d, wdqT_d, "d:wdqT", t, coff, ww)
                  for t in range(NCF)
                  for coff, ww in ((0, 1024), (1024, 1024), (2048, 768))]
            out, gi, di = [], 0, 0
            while gi < len(gu) or di < len(wd):
                for _ in range(3):
                    if gi < len(gu):
                        out.append(gu[gi] + (gi % 2,))
                        gi += 1
                if di < len(wd):
                    out.append(wd[di] + (di % 2,))
                    di += 1
            return out

        def w_plan(job):
            w_in, w_out, wok, t, coff, width, b = job
            Ft = width
            wtk, wqk_, wtqk = "wt_w%d" % b, "wq_w%d" % b, "wtq_w%d" % b
            wta, wqa, wtqa = wt_w[b], wq_w[b], wtq_w[b]
            ptk = "PT%d" % b
            p = Plan()
            p.dma(DSY, wta[:, 0:width],
                  w_in[t * 128:(t + 1) * 128, coff:coff + width], wr=[wtk])
            quant_tile(p, wtk, wta[:, :Ft], wqk_, wqa[:, :Ft], Ft,
                       scrw[b], pre="w%d_" % b)
            nch = Ft // 128
            for cb in range(0, nch, 8):
                cn = min(8, nch - cb)
                for c in range(cn):
                    p.do(T, lambda c=c, cb=cb, b=b, wqa=wqa: T.stream.transpose(
                        PTs[b][:, c * 128:(c + 1) * 128],
                        wqa[:, (cb + c) * 128:(cb + c + 1) * 128], ident_s[:]),
                        rd=[wqk_, "params"], wr=[ptk], sync=(c == cn - 1))
                p.do(V, lambda cb=cb, cn=cn, b=b, wtqa=wtqa: V.stream.tensor_copy(
                    wtqa[:, cb * 128:(cb + cn) * 128], PTs[b][:, 0:cn * 128]),
                    rd=[ptk], wr=[wtqk])
            ncpt = width // 128
            p.dma(DSY, w_out.rearrange("(c p) n -> p c n", p=128)[:, coff // 128:(coff + width) // 128, t * 128:(t + 1) * 128],
                  wtqa.rearrange("p (c t) -> p c t", t=128)[:, 0:ncpt, :],
                  rd=[wtqk], wr=[wok])
            return p

        w_iter = iter(w_jobs())

        def next_wplan():
            if Ser.WQ_FRONT:
                return None
            job = next(w_iter, None)
            return w_plan(job) if job is not None else None

        if Ser.WQ_FRONT:
            while True:
                job = next(w_iter, None)
                if job is None:
                    break
                emit(ser, w_plan(job))

        xt = [psb("xt0", [128, D], F32), psb("xt1", [128, D], F32)]
        xsq = [psb("xsq0", [128, D], F32), psb("xsq1", [128, D], F32)]
        ln1b = [psb("ln1b0", [128, D], BF16), psb("ln1b1", [128, D], BF16)]
        ln1T = psb("ln1T", [128, NCF * S], BF16)
        v_s = psb("v_s", [128, NT * D], BF16)
        attn_b = psb("attn_b", [128, NT * 512], BF16)
        simb = [psb("simb0", [128, 512], F32), psb("simb1", [128, 512], F32)]
        uu = [psb("uu0", [128, 512], F32), psb("uu1", [128, 512], F32)]
        sg_s = [psb("sg0", [128, 512], F32), psb("sg1", [128, 512], F32)]
        x2t = [psb("x2t0", [128, D], F32), psb("x2t1", [128, D], F32)]

        for s in range(NSEQ):
            c0_, c1_ = strip_col(s, 0, 0), strip_col(s, 1, 0)

            def stats_plan(tt, b):
                p = Plan()
                xk, xa = "xt%d" % b, xt[b]
                qk_, qa = "xsq%d" % b, xsq[b]
                p.dma(DSY, xa[:], x_d[s, tt * 128:(tt + 1) * 128, :], wr=[xk])
                p.do(V, lambda: V.stream.reduce_sum(
                    strip[:, strip_col(s, 0, tt):strip_col(s, 0, tt) + 1],
                    xa[:], AX.X), rd=[xk], wr=["strip"])
                p.do(Sc, lambda: Sc.stream.activation(qa[:], xa[:], AF.Square),
                     rd=[xk], wr=[qk_])
                p.do(V, lambda: V.stream.reduce_sum(
                    strip[:, strip_col(s, 1, tt):strip_col(s, 1, tt) + 1],
                    qa[:], AX.X), rd=[qk_], wr=["strip"])
                return p

            for tt in range(0, NT, 2):
                emit(ser, stats_plan(tt, 0), stats_plan(tt + 1, 1))

            m1 = strip[:, c0_:c0_ + NT]
            v1 = strip[:, c1_:c1_ + NT]
            ser.do(V, lambda m1=m1: V.stream.tensor_scalar(m1, m1, c32(1.0 / D), None, OP.mult), rd=["strip"], wr=["strip"])
            ser.do(V, lambda v1=v1: V.stream.tensor_scalar(v1, v1, c32(1.0 / D), None, OP.mult), rd=["strip"], wr=["strip"])
            ser.do(V, lambda m1=m1: V.stream.tensor_tensor(xsq[0][:, 0:NT], m1, m1, OP.mult), rd=["strip"], wr=["xsq0"])
            ser.do(V, lambda v1=v1: V.stream.tensor_sub(v1, v1, xsq[0][:, 0:NT]), rd=["strip", "xsq0"], wr=["strip"])
            ser.do(V, lambda v1=v1: V.stream.tensor_scalar(v1, v1, c32(1e-5), None, OP.add), rd=["strip"], wr=["strip"])
            ser.do(Sc, lambda v1=v1: Sc.stream.activation(v1, v1, AF.Sqrt), rd=["strip"], wr=["strip"])
            ser.do(V, lambda v1=v1: V.stream.reciprocal(v1, v1), rd=["strip"], wr=["strip"])
            ser.do(V, lambda m1=m1, v1=v1: V.stream.tensor_mul(m1, m1, v1), rd=["strip"], wr=["strip"])
            ser.do(V, lambda m1=m1: V.stream.tensor_scalar(m1, m1, -1.0, None, OP.mult), rd=["strip"], wr=["strip"])

            def norm_plan(tt, b):
                p = Plan()
                xk, xa = "xt%d" % b, xt[b]
                lk, la = "ln1b%d" % b, ln1b[b]
                ptk = "PT%d" % b
                p.dma(DSY, xa[:], x_d[s, tt * 128:(tt + 1) * 128, :], wr=[xk])
                p.do(V, lambda: V.stream.tensor_scalar(
                    la[:], xa[:], strip[:, c1_ + tt:c1_ + tt + 1],
                    strip[:, c0_ + tt:c0_ + tt + 1], OP.mult, OP.add),
                    rd=[xk, "strip"], wr=[lk])
                for c in range(NCF):
                    p.do(T, lambda c=c: T.stream.transpose(
                        PTs[b][:, c * 128:(c + 1) * 128],
                        la[:, c * 128:(c + 1) * 128], ident_s[:]),
                        rd=[lk, "params"], wr=[ptk], sync=(c == NCF - 1))
                p.do(V, lambda: V.stream.tensor_copy(
                    ln1T.rearrange("p (c t) -> p c t", c=NCF)[:, :, tt * 128:(tt + 1) * 128],
                    PTs[b].rearrange("p (c t) -> p c t", c=NCF)),
                    rd=[ptk], wr=["ln1T"])
                return p

            for tt in range(0, NT, 2):
                emit(ser, norm_plan(tt, 0), norm_plan(tt + 1, 1))

            def qk_plan(tb, b):
                p = Plan()
                PP, pk, _, _ = BKS[b]
                uk, ua = "uu%d" % b, uu[b]
                gk, ga = "sg%d" % b, sg_s[b]
                for c in range(NCF):
                    p.do(T, lambda c=c: T.stream.matmul(
                        PP[:, :], wqk_s[:, c * E:(c + 1) * E],
                        ln1T[:, c * S + tb * 512: c * S + (tb + 1) * 512],
                        start=(c == 0), stop=(c == NCF - 1)),
                        rd=["wqk_s", "ln1T"], wr=[pk], sync=(c == NCF - 1))
                p.do(V, lambda: V.stream.tensor_scalar(ua[:], PP[:, :], bqk_s[:], None, OP.add),
                     rd=[pk, "params"], wr=[uk])
                p.do(Sc, lambda: Sc.stream.activation(ga[:], ua[:], AF.Sigmoid),
                     rd=[uk], wr=[gk])
                p.do(V, lambda: V.stream.tensor_mul(ua[:], ua[:], ga[:]),
                     rd=[uk, gk], wr=[uk])
                p.do(V, lambda: V.stream.tensor_scalar(
                    q_s[:, tb * 512:(tb + 1) * 512], ua[:], g0s[:], b0s[:],
                    OP.mult, OP.add), rd=[uk, "params2"], wr=["q_s"])
                p.do(V, lambda: V.stream.tensor_scalar(
                    k_s[:, tb * 512:(tb + 1) * 512], ua[:], g1s[:], b1s[:],
                    OP.mult, OP.add), rd=[uk, "params2"], wr=["k_s"])
                return p

            tbs = list(range(S // 512))
            for ii in range(0, len(tbs), 2):
                plans = [qk_plan(tbs[ii], 0)]
                if ii + 1 < len(tbs):
                    plans.append(qk_plan(tbs[ii + 1], 1))
                emit(ser, *plans)

            def v_plan(tt, b):
                p = Plan()
                P0, k0, P1, k1 = BKS[b]
                uk, ua = "uu%d" % b, uu[b]
                gk, ga = "sg%d" % b, sg_s[b]
                for hh in range(2):
                    PP, pk = (P0, k0) if hh == 0 else (P1, k1)
                    for c in range(NCF):
                        p.do(T, lambda c=c, PP=PP, hh=hh: T.stream.matmul(
                            PP[:, :],
                            ln1T[:, c * S + tt * 128: c * S + (tt + 1) * 128],
                            wv_s[:, c * D + hh * 512: c * D + (hh + 1) * 512],
                            start=(c == 0), stop=(c == NCF - 1)),
                            rd=["ln1T", "wv_s"], wr=[pk], sync=(c == NCF - 1))
                    p.do(V, lambda PP=PP, pk=pk, hh=hh: V.stream.tensor_add(
                        ua[:], PP[:, :], bv_rep[:, hh * 512:(hh + 1) * 512]),
                        rd=[pk, "bv_rep"], wr=[uk])
                    p.do(Sc, lambda: Sc.stream.activation(ga[:], ua[:], AF.Sigmoid),
                         rd=[uk], wr=[gk])
                    p.do(V, lambda hh=hh: V.stream.tensor_mul(
                        v_s[:, tt * D + hh * 512: tt * D + (hh + 1) * 512],
                        ua[:], ga[:]), rd=[uk, gk], wr=["v_s"])
                return p

            for tt in range(0, NT, 2):
                emit(ser, v_plan(tt, 0), v_plan(tt + 1, 1))

            for qst in range(NQST):
                kmax = qst * 4 + 4

                def sim_plan(kb, b):
                    p = Plan()
                    PP, pk, _, _ = BKS[b]
                    sk, sa = "simb%d" % b, simb[b]
                    ek, ea = sk, sa
                    p.do(T, lambda: T.stream.matmul(
                        PP[:, :], k_s[:, kb * 128:(kb + 1) * 128],
                        q_s[:, qst * 512:(qst + 1) * 512],
                        start=True, stop=True), rd=["k_s", "q_s"], wr=[pk])
                    t0 = qst * 4 - kb + 3
                    p.do(V, lambda: V.stream.tensor_add(
                        sa[:], PP[:, :], biasT[:, t0 * 128:t0 * 128 + 512]),
                        rd=[pk, "biasT"], wr=[sk])
                    p.do(Sc, lambda: Sc.stream.activation(
                        ea[:], sa[:], AF.Erf, bias=ERF_BIAS, scale=ERF_SCALE),
                        rd=[sk], wr=[ek])
                    p.do(V, lambda: V.stream.tensor_scalar(
                        attn_b[:, kb * 512:(kb + 1) * 512], ea[:],
                        0.5, 0.5, OP.mult, OP.add), rd=[ek], wr=["attn_b"])
                    return p

                for kb in range(0, kmax, 2):
                    plans = [sim_plan(kb, 0)]
                    if kb + 1 < kmax:
                        plans.append(sim_plan(kb + 1, 1))
                    wp = next_wplan()
                    if wp is not None:
                        plans.append(wp)
                    emit(ser, *plans)

                def av_plan(j, b):
                    p = Plan()
                    qt = qst * 4 + j
                    P0, k0, P1, k1 = BKS[b]
                    xk, xa = "xt%d" % b, xt[b]
                    x2k, x2a = "x2t%d" % b, x2t[b]
                    qk_, qa = "xsq%d" % b, xsq[b]
                    p.dma(DSY, xa[:], x_d[s, qt * 128:(qt + 1) * 128, :], wr=[xk])
                    for kb in range(qt + 1):
                        for hh in range(2):
                            PP, pk = (P0, k0) if hh == 0 else (P1, k1)
                            p.do(T, lambda kb=kb, PP=PP, hh=hh: T.stream.matmul(
                                PP[:, :],
                                attn_b[:, kb * 512 + j * 128: kb * 512 + (j + 1) * 128],
                                v_s[:, kb * D + hh * 512: kb * D + (hh + 1) * 512],
                                start=(kb == 0), stop=(kb == qt)),
                                rd=["attn_b", "v_s"], wr=[pk], sync=(kb == qt))
                    p.do(V, lambda: V.stream.tensor_add(x2a[:, 0:512], P0[:, :], xa[:, 0:512]),
                         rd=[k0, xk], wr=[x2k])
                    p.do(V, lambda: V.stream.tensor_add(x2a[:, 512:1024], P1[:, :], xa[:, 512:1024]),
                         rd=[k1, xk], wr=[x2k])
                    p.do(V, lambda qt=qt: V.stream.reduce_sum(
                        strip[:, strip_col(s, 2, qt):strip_col(s, 2, qt) + 1],
                        x2a[:], AX.X), rd=[x2k], wr=["strip"])
                    p.do(Sc, lambda: Sc.stream.activation(qa[:], x2a[:], AF.Square),
                         rd=[x2k], wr=[qk_])
                    p.do(V, lambda qt=qt: V.stream.reduce_sum(
                        strip[:, strip_col(s, 3, qt):strip_col(s, 3, qt) + 1],
                        qa[:], AX.X), rd=[qk_], wr=["strip"])
                    p.dma(DSY, x2s_d[s, qt * 128:(qt + 1) * 128, :], x2a[:],
                          rd=[x2k], wr=["d:x2s"])
                    return p

                for j in range(0, 4, 2):
                    plans = [av_plan(j, 0), av_plan(j + 1, 1)]
                    if not Ser.WQ_NOAV:
                        wp = next_wplan()
                        if wp is not None:
                            plans.append(wp)
                    emit(ser, *plans)

            c2_, c3_ = strip_col(s, 2, 0), strip_col(s, 3, 0)
            m2 = strip[:, c2_:c2_ + NT]
            v2 = strip[:, c3_:c3_ + NT]
            ser.do(V, lambda m2=m2: V.stream.tensor_scalar(m2, m2, c32(1.0 / D), None, OP.mult), rd=["strip"], wr=["strip"])
            ser.do(V, lambda v2=v2: V.stream.tensor_scalar(v2, v2, c32(1.0 / D), None, OP.mult), rd=["strip"], wr=["strip"])
            ser.do(V, lambda m2=m2: V.stream.tensor_tensor(xsq[0][:, 0:NT], m2, m2, OP.mult), rd=["strip"], wr=["xsq0"])
            ser.do(V, lambda v2=v2: V.stream.tensor_sub(v2, v2, xsq[0][:, 0:NT]), rd=["strip", "xsq0"], wr=["strip"])
            ser.do(V, lambda v2=v2: V.stream.tensor_scalar(v2, v2, c32(1e-5), None, OP.add), rd=["strip"], wr=["strip"])
            ser.do(Sc, lambda v2=v2: Sc.stream.activation(v2, v2, AF.Sqrt), rd=["strip"], wr=["strip"])
            ser.do(V, lambda v2=v2: V.stream.reciprocal(v2, v2), rd=["strip"], wr=["strip"])
            ser.do(V, lambda m2=m2, v2=v2: V.stream.tensor_mul(m2, m2, v2), rd=["strip"], wr=["strip"])
            ser.do(V, lambda m2=m2: V.stream.tensor_scalar(m2, m2, -1.0, None, OP.mult), rd=["strip"], wr=["strip"])

        while True:
            wp = next_wplan()
            if wp is None:
                break
            emit(ser, wp)

        if "x2tap" in tap_d:
            ser.dma(DSY, xt[0][:], x2s_d[0, 0:128, :], rd=["d:x2s"], wr=["xt0"])
            ser.dma(DSY, tap_d["x2tap"][:], xt[0][:], rd=["xt0"])
        if "v" in tap_d:
            ser.dma(DSY, tap_d["v"][:], v_s[:], rd=["v_s"])
        if "q" in tap_d:
            ser.dma(DSY, tap_d["q"][:], q_s[:], rd=["q_s"])
        if "k" in tap_d:
            ser.dma(DSY, tap_d["k"][:], k_s[:], rd=["k_s"])
        if "attnb" in tap_d:
            ser.dma(DSY, tap_d["attnb"][:], attn_b[:], rd=["attn_b"])
    ser.fence()

    SL = [512] * (H // 512) + ([H % 512] if H % 512 else [])
    with contextlib.ExitStack() as ph:
        def psb(name, shape, dt):
            return ph.enter_context(nc.sbuf_tensor(name, shape, dt)).ap()

        x2g = psb("x2g", [128, GSZ * D], F32)
        ln2f = [psb("ln2f0", [128, D], F32), psb("ln2f1", [128, D], F32)]
        ln2qb = [psb("ln2qb0", [128, D], BF16), psb("ln2qb1", [128, D], BF16)]
        ln2qT = psb("ln2qT", [128, NCF * GSZ * 128], BF16)
        wgs = [psb("wgs0", [128, NCF * 512], BF16), psb("wgs1", [128, NCF * 512], BF16)]
        wus = [psb("wus0", [128, NCF * 512], BF16), psb("wus1", [128, NCF * 512], BF16)]
        hqT = psb("hqT", [128, NCH * GSZ * 128], BF16)
        wdq_s = psb("wdq_s", [128, NCH * D], BF16)
        hfull = [psb("hfull0", [128, 1024], F32), psb("hfull1", [128, 1024], F32)]
        sgf = [psb("sgf0", [128, 1024], F32), psb("sgf1", [128, 1024], F32)]
        otile = [psb("otile0", [128, D], F32), psb("otile1", [128, D], F32)]
        scr3 = []
        for b in range(2):
            scr3.append({
                "bm": psb("p3%d_bm" % b, [128, D // 16], F32),
                "s_": psb("p3%d_s_" % b, [128, D // 16], F32),
                "rc": psb("p3%d_rc" % b, [128, D // 16], F32),
                "rb2": psb("p3%d_rb2" % b, [128, D // 16], F32),
                "x2": psb("p3%d_x2" % b, [128, D], F32),
                "c0": psb("p3%d_c0" % b, [128, D], BF16),
                "c1": psb("p3%d_c1" % b, [128, D], BF16),
                "c2": psb("p3%d_c2" % b, [128, D], BF16),
                "sg": psb("p3%d_sg" % b, [128, D], BF16)})

        ser.dma(DSY, wdq_s.rearrange("p (c n) -> p c n", c=NCH),
                wdqT_d.rearrange("(c p) n -> p c n", p=128),
                rd=["d:wdqT"], wr=["wdq_s"])

        def load_slab(sl_i):
            off = 512 * sl_i
            slw = SL[sl_i]
            b = sl_i % 2
            ser.dma(DSY, wgs[b].rearrange("p (c n) -> p c n", c=NCF)[:, :, 0:slw],
                    wgqT_d.rearrange("(c p) n -> p c n", p=128)[:, :, off:off + slw],
                    rd=["d:wgqT"], wr=["wgs%d" % b])
            ser.dma(DSY, wus[b].rearrange("p (c n) -> p c n", c=NCF)[:, :, 0:slw],
                    wuqT_d.rearrange("(c p) n -> p c n", p=128)[:, :, off:off + slw],
                    rd=["d:wuqT"], wr=["wus%d" % b])

        for grp in range(NGRP):
            load_slab(0)

            def ln2_plan(i, b):
                p = Plan()
                gt = grp * GSZ + i
                s, qt = divmod(gt, NT)
                xgk = "x2g%d" % i
                lk, la = "ln2f%d" % b, ln2f[b]
                qk_, qa = "ln2qb%d" % b, ln2qb[b]
                p.dma(DSY, x2g[:, i * D:(i + 1) * D],
                      x2s_d[s, qt * 128:(qt + 1) * 128, :],
                      rd=["d:x2s"], wr=[xgk])
                p.do(V, lambda: V.stream.tensor_scalar(
                    la[:], x2g[:, i * D:(i + 1) * D],
                    strip[:, strip_col(s, 3, qt):strip_col(s, 3, qt) + 1],
                    strip[:, strip_col(s, 2, qt):strip_col(s, 2, qt) + 1],
                    OP.mult, OP.add), rd=[xgk, "strip"], wr=[lk])
                quant_tile(p, lk, la[:], qk_, qa[:], D, scr3[b], pre="p3%d" % b)
                for c in range(NCF):
                    p.do(T, lambda c=c: T.stream.transpose(
                        PTs[b][:, c * 128:(c + 1) * 128],
                        qa[:, c * 128:(c + 1) * 128], ident_s[:]),
                        rd=[qk_, "params"], wr=["PT%d" % b], sync=(c == NCF - 1))
                p.do(V, lambda: V.stream.tensor_copy(
                    ln2qT.rearrange("p (c g) -> p c g", c=NCF)[:, :, i * 128:(i + 1) * 128],
                    PTs[b].rearrange("p (c t) -> p c t", c=NCF)),
                    rd=["PT%d" % b], wr=["ln2qT%d" % i])
                return p

            for i in range(0, GSZ, 2):
                emit(ser, ln2_plan(i, 0), ln2_plan(i + 1, 1))

            for sl_i, slw in enumerate(SL):
                sb_ = sl_i % 2
                wgk, wuk = "wgs%d" % sb_, "wus%d" % sb_
                off = 512 * sl_i
                if sl_i + 1 < len(SL):
                    load_slab(sl_i + 1)

                def h_plan(i0, b, sl_i=sl_i, slw=slw, sb_=sb_, wgk=wgk, wuk=wuk, off=off):
                    p = Plan()
                    P0, k0, P1, k1 = BKS[b]
                    gk, ga = "sgf%d" % b, sgf[b]
                    hk, ha = "hfull%d" % b, hfull[b]
                    qk_, qa = "ln2qb%d" % b, ln2qb[b]
                    for u in range(2):
                        i = i0 + u
                        uo = u * slw
                        for c in range(NCF):
                            p.do(T, lambda c=c, i=i: T.stream.matmul(
                                P0[:, 0:slw],
                                ln2qT[:, c * GSZ * 128 + i * 128: c * GSZ * 128 + (i + 1) * 128],
                                wgs[sb_][:, c * 512: c * 512 + slw],
                                start=(c == 0), stop=(c == NCF - 1)),
                                rd=["ln2qT%d" % i, wgk], wr=[k0], sync=(c == NCF - 1))
                            p.do(T, lambda c=c, i=i: T.stream.matmul(
                                P1[:, 0:slw],
                                ln2qT[:, c * GSZ * 128 + i * 128: c * GSZ * 128 + (i + 1) * 128],
                                wus[sb_][:, c * 512: c * 512 + slw],
                                start=(c == 0), stop=(c == NCF - 1)),
                                rd=["ln2qT%d" % i, wuk], wr=[k1], sync=(c == NCF - 1))
                        p.do(Sc, lambda uo=uo: Sc.stream.activation(
                            ga[:, uo:uo + slw], P0[:, 0:slw], AF.Sigmoid), rd=[k0], wr=[gk])
                        p.do(V, lambda uo=uo: V.stream.tensor_mul(
                            ga[:, uo:uo + slw], ga[:, uo:uo + slw], P0[:, 0:slw]), rd=[gk, k0], wr=[gk])
                        p.do(V, lambda uo=uo: V.stream.tensor_mul(
                            ha[:, uo:uo + slw], ga[:, uo:uo + slw], P1[:, 0:slw]), rd=[gk, k1], wr=[hk])
                    p2 = Plan()
                    quant_tile(p2, hk, ha[:, 0:2 * slw], qk_, qa[:, 0:2 * slw], 2 * slw,
                               scr3[b], pre="p3%d" % b)
                    for c in range(2 * slw // 128):
                        p2.do(T, lambda c=c: T.stream.transpose(
                            PTs[b][:, c * 128:(c + 1) * 128],
                            qa[:, c * 128:(c + 1) * 128], ident_s[:]),
                            rd=[qk_, "params"], wr=["PT%d" % b], sync=(c == 2 * slw // 128 - 1))
                    for u in range(2):
                        i = i0 + u
                        p2.do(V, lambda u=u, i=i: V.stream.tensor_copy(
                            hqT.rearrange("p (c g) -> p c g", c=NCH)[:, off // 128:(off + slw) // 128, i * 128:(i + 1) * 128],
                            PTs[b].rearrange("p (c t) -> p c t", t=128)[:, u * (slw // 128):(u + 1) * (slw // 128), :]),
                            rd=["PT%d" % b], wr=["hqT%d" % i])
                    p.ops.extend(p2.ops)
                    return p

                emit(ser, h_plan(0, 0), h_plan(2, 1))

            def dn_plan(i, b):
                p = Plan()
                gt = grp * GSZ + i
                s, qt = divmod(gt, NT)
                P0, k0, P1, k1 = BKS[b]
                ok, oa = "otile%d" % b, otile[b]
                for hh in range(2):
                    PP, pk = (P0, k0) if hh == 0 else (P1, k1)
                    for hc in range(NCH):
                        p.do(T, lambda hc=hc, PP=PP, hh=hh: T.stream.matmul(
                            PP[:, :],
                            hqT[:, hc * GSZ * 128 + i * 128: hc * GSZ * 128 + (i + 1) * 128],
                            wdq_s[:, hc * D + hh * 512: hc * D + (hh + 1) * 512],
                            start=(hc == 0), stop=(hc == NCH - 1)),
                            rd=["hqT%d" % i, "wdq_s"], wr=[pk], sync=(hc == NCH - 1))
                p.do(V, lambda: V.stream.tensor_add(
                    oa[:, 0:512], P0[:, :], x2g[:, i * D:i * D + 512]),
                    rd=[k0, "x2g%d" % i], wr=[ok])
                p.do(V, lambda: V.stream.tensor_add(
                    oa[:, 512:1024], P1[:, :], x2g[:, i * D + 512:(i + 1) * D]),
                    rd=[k1, "x2g%d" % i], wr=[ok])
                p.dma(DSY, out_d[s, qt * 128:(qt + 1) * 128, :], oa[:],
                      rd=[ok], wr=["d:out"])
                return p

            for i in range(0, GSZ, 2):
                emit(ser, dn_plan(i, 0), dn_plan(i + 1, 1))

        for nm, wsrc in (("wgq", wgqT_d), ("wuq", wuqT_d)):
            if nm in tap_d:
                for off in range(0, H, 512):
                    sw = min(512, H - off)
                    ser.dma(DSY, wgs[0].rearrange("p (c n) -> p c n", c=NCF)[:, :, 0:sw],
                            wsrc.rearrange("(c p) n -> p c n", p=128)[:, :, off:off + sw],
                            rd=["d:" + nm.replace("q", "q").replace("wgq", "wgqT").replace("wuq", "wuqT")], wr=["wgs0"])
                    ser.dma(DSY, tap_d[nm].rearrange("(c p) n -> p c n", p=128)[:, :, off:off + sw],
                            wgs[0].rearrange("p (c n) -> p c n", c=NCF)[:, :, 0:sw],
                            rd=["wgs0"])
        if "wdq" in tap_d:
            for off in range(0, D, 512):
                ser.dma(DSY, wdq_s.rearrange("p (c n) -> p c n", c=NCH)[:, :, 0:512],
                        wdqT_d.rearrange("(c p) n -> p c n", p=128)[:, :, off:off + 512],
                        rd=["d:wdqT"], wr=["wdq_s"])
                ser.dma(DSY, tap_d["wdq"].rearrange("(c p) n -> p c n", p=128)[:, :, off:off + 512],
                        wdq_s.rearrange("p (c n) -> p c n", c=NCH)[:, :, 0:512],
                        rd=["wdq_s"])

    ser.finish()
    stack.close()
    return nc


def kernel(L0_attn_x, L0_attn_Wqk, L0_attn_bqk, L0_attn_gamma, L0_attn_beta,
           L0_attn_Wv, L0_attn_bv, L0_attn_rel_emb, L0_ffn_w_gate,
           L0_ffn_w_up, L0_ffn_w_down):
    from concourse.bass_utils import run_bass_kernel_spmd
    x = np.ascontiguousarray(np.asarray(L0_attn_x, dtype=np.float32))
    B, S_, D_ = x.shape
    n_cores = 8
    nseq = B // n_cores
    nc = bass.Bass("TRN2", target_bir_lowering=False)
    build(nc, S=S_, NSEQ=nseq)
    common = {
        "wqk": np.ascontiguousarray(L0_attn_Wqk, dtype=np.float32),
        "bqk": np.ascontiguousarray(np.asarray(L0_attn_bqk, dtype=np.float32).reshape(E, 1)),
        "gam": np.ascontiguousarray(L0_attn_gamma, dtype=np.float32),
        "bet": np.ascontiguousarray(L0_attn_beta, dtype=np.float32),
        "wv": np.ascontiguousarray(L0_attn_Wv, dtype=np.float32),
        "bv": np.ascontiguousarray(np.asarray(L0_attn_bv, dtype=np.float32).reshape(1, D)),
        "rel": np.ascontiguousarray(L0_attn_rel_emb, dtype=np.float32),
        "wg": np.ascontiguousarray(L0_ffn_w_gate, dtype=np.float32),
        "wu": np.ascontiguousarray(L0_ffn_w_up, dtype=np.float32),
        "wd": np.ascontiguousarray(L0_ffn_w_down, dtype=np.float32),
    }
    in_maps = [dict(common, x=np.ascontiguousarray(x[c * nseq:(c + 1) * nseq]))
               for c in range(n_cores)]
    res = run_bass_kernel_spmd(nc, in_maps, core_ids=list(range(n_cores)))
    return np.concatenate([np.asarray(r["out"]) for r in res.results], axis=0)


# version 09
# speedup vs baseline: 1.1051x; 1.1051x over previous
import math
import contextlib
import numpy as np
import ml_dtypes
import concourse.bass as bass
import concourse.mybir as mybir
from concourse.ap import AP

F32 = mybir.dt.float32
BF16 = mybir.dt.bfloat16
OP = mybir.AluOpType
AF = mybir.ActivationFunctionType
AX = mybir.AxisListType
f32 = np.float32
bf16 = ml_dtypes.bfloat16


def c32(v):
    return float(np.float32(v))


D = 1024
E = 128
H = 2816
NCF = D // 128
NCH = H // 128

ERF_SCALE = f32(math.sqrt(2.0 / math.pi))
ERF_BIAS = f32(-math.sqrt(0.5) * math.sqrt(2.0 / math.pi))
BIG = f32(1e30)
QTHR = [0.0625, 0.5625, 1.5625, 3.0625, 6.25, 12.25, 25.0]
NEG_MASK = -1e30


def bucket_delta(S, NUM_BUCKETS=32, MAX_DISTANCE=128):
    n = np.arange(S)
    max_exact = NUM_BUCKETS // 2
    is_small = n < max_exact
    nf = np.maximum(n, 1).astype(np.float32)
    vl = max_exact + (np.log(nf / max_exact) / math.log(MAX_DISTANCE / max_exact)
                      * (NUM_BUCKETS - max_exact)).astype(np.int32)
    vl = np.minimum(vl, NUM_BUCKETS - 1)
    return np.where(is_small, n, vl).astype(np.int64)


def build_consts(S):
    ident = np.eye(128, dtype=np.float32).astype(bf16)
    W = 512 + S
    s33 = np.zeros((33, W), dtype=np.float32)
    s33[32, :512] = NEG_MASK
    bd = bucket_delta(S)
    s33[bd, np.arange(512, 512 + S)] = np.float32(128.0 ** 0.5)
    return ident, s33, W


class Rsrc:
    __slots__ = ("w_q", "w_seq", "w_vc", "readers")

    def __init__(self):
        self.w_q = None
        self.w_seq = 0
        self.w_vc = None
        self.readers = {}


class Ser:

    class Q:
        __slots__ = ("name", "stream", "sem", "count", "dma", "pending",
                     "subs", "sub_uses", "K")

        def __init__(q, name, stream, sem, dma, subs=None):
            q.name = name
            q.stream = stream
            q.sem = sem
            q.count = 0
            q.dma = dma
            q.pending = False
            q.subs = subs or []
            q.sub_uses = [0] * len(q.subs)
            q.K = len(q.subs)

        def wait_spec(q, k):
            j = (k - 1) % q.K
            return q.subs[j], 16 * ((k - 1) // q.K + 1)

    PARANOID = False
    PARA_SKIP = None
    DUMP = None
    NO_COVER = False

    def __init__(self, nc):
        self.nc = nc
        self.res = {}
        self.wm = {}
        self.idx = 0
        self.fence_counts = {}
        mkq = lambda name, stream, dma: Ser.Q(name, stream, nc.alloc_semaphore(name="sem_" + name), dma)

        def mkdq(name, stream, K):
            subs = [nc.alloc_semaphore(name="sem_%s%d" % (name, j)) for j in range(K)]
            return Ser.Q(name, stream, None, True, subs=subs)

        self.T = mkq("T", nc.tensor, False)
        self.V = mkq("V", nc.vector, False)
        self.S = mkq("S", nc.scalar, False)
        self.G = mkq("G", nc.gpsimd, False)
        self.DSY = mkdq("DSY", nc.sync, 8)
        self.DACT = mkdq("DACT", nc.scalar, 2)
        self.DGP = mkdq("DGP", nc.gpsimd, 2)
        self.queues = [self.T, self.V, self.S, self.G, self.DSY, self.DACT,
                       self.DGP]
        self.fq = {q: mkq("F" + q.name, q.stream, False)
                   for q in (self.DSY, self.DACT, self.DGP)}

    def _r(self, key):
        r = self.res.get(key)
        if r is None:
            r = self.res[key] = Rsrc()
        return r

    def do(self, q, fn, rd=(), wr=(), sync=True):
        deps = {}

        def add(dq, ds, dvc):
            cur = deps.get(dq)
            if cur is None or cur[0] < ds:
                deps[dq] = (ds, dvc)

        for k in rd:
            r = self.res.get(k)
            if r is not None and r.w_q is not None:
                add(r.w_q, r.w_seq, r.w_vc)
        for k in wr:
            r = self.res.get(k)
            if r is None:
                continue
            if r.w_q is not None:
                add(r.w_q, r.w_seq, r.w_vc)
            for dq, (ds, dvc) in r.readers.items():
                add(dq, ds, dvc)
        for dq, ds in self.fence_counts.items():
            add(dq, ds, None)
        paranoid = Ser.PARANOID
        if Ser.PARA_SKIP is not None:
            lo, hi = Ser.PARA_SKIP
            paranoid = not (lo <= self.idx < hi)
        self.idx += 1
        if paranoid:
            for dq in self.queues:
                if dq is not q and dq.count and not dq.pending:
                    add(dq, dq.count, None)
        def covers(dq, claim, need):
            if dq.dma:
                return claim == need or claim >= need + dq.K
            return claim >= need

        items = list(deps.items())
        kept = {}
        for dq, (ds, dvc) in items:
            covered = False
            if not Ser.NO_COVER:
                for dq2, (ds2, dvc2) in items:
                    if dq2 is dq or dvc2 is None:
                        continue
                    if covers(dq, dvc2.get(dq, 0), ds):
                        covered = True
                        break
            if not covered:
                kept[dq] = (ds, dvc)
        need_drain = False

        def emit_wait(dq, ds):
            wmk = (q.stream, dq)
            if dq.dma:
                prev = self.wm.get(wmk, 0)
                if not (prev and covers(dq, prev, ds)):
                    sem, thr = dq.wait_spec(ds)
                    q.stream.wait_ge(sem, thr)
                    if ds > prev:
                        self.wm[wmk] = ds
            else:
                if self.wm.get(wmk, 0) < ds:
                    q.stream.wait_ge(dq.sem, ds)
                    self.wm[wmk] = ds

        for dq, (ds, dvc) in kept.items():
            if dq is q:
                if q.dma:
                    emit_wait(dq, ds)
                elif q is self.V and ds >= q.count:
                    need_drain = True
                continue
            if dq.pending and ds > dq.count:
                raise RuntimeError(f"cross-queue dep on pending {dq.name}")
            emit_wait(dq, ds)
        if Ser.DUMP and Ser.DUMP[0] <= self.idx - 1 < Ser.DUMP[1]:
            import sys
            print("[dump]", self.idx - 1, q.name, "rd", list(rd), "wr", list(wr),
                  "kept", {dq.name: ds for dq, (ds, _) in kept.items()},
                  "alldeps", {dq.name: ds for dq, (ds, _) in deps.items()},
                  "drain", need_drain, file=sys.stderr)
        if need_drain:
            q.stream.drain()
        if q.dma:
            jj = q.count % q.K
            if q.sub_uses[jj]:
                q.stream.wait_ge(q.subs[jj], 16 * q.sub_uses[jj])
        r = fn()
        if sync:
            if q.dma:
                jj = q.count % q.K
                r.then_inc(q.subs[jj], 16)
                q.sub_uses[jj] += 1
                q.count += 1
            else:
                r.then_inc(q.sem, 1)
                q.count += 1
            q.pending = False
            seq = q.count
        else:
            q.pending = True
            seq = q.count + 1
        myvc = {q: seq}
        for dq, (ds, dvc) in deps.items():
            if dq is not q and myvc.get(dq, 0) < ds:
                myvc[dq] = ds
            if dvc is not None:
                for q3, s3 in dvc.items():
                    if myvc.get(q3, 0) < s3:
                        myvc[q3] = s3
        for k in wr:
            rs = self._r(k)
            rs.w_q = q
            rs.w_seq = seq
            rs.w_vc = myvc
            rs.readers = {}
        for k in rd:
            rs = self._r(k)
            rs.readers[q] = (seq, myvc)
        return r

    def fence(self):
        for q in self.queues:
            if q.dma:
                if not q.count:
                    continue
                for j in range(q.K):
                    if q.sub_uses[j]:
                        q.stream.wait_ge(q.subs[j], 16 * q.sub_uses[j])
                fq = self.fq[q]
                q.stream.sem_inc(fq.sem, 1)
                fq.count += 1
                self.fence_counts[fq] = fq.count
            elif q.count:
                self.fence_counts[q] = q.count

    def dma(self, q, out, in_, rd=(), wr=()):
        return self.do(q, lambda: q.stream.dma_start(out, in_), rd=rd, wr=wr)

    def finish(self):
        for q in self.queues:
            if not q.count:
                continue
            if q.dma:
                for j in range(q.K):
                    if q.sub_uses[j]:
                        self.nc.sync.wait_ge(q.subs[j], 16 * q.sub_uses[j])
            else:
                self.nc.sync.wait_ge(q.sem, q.count)


class Plan:

    def __init__(self):
        self.ops = []

    def do(self, q, fn, rd=(), wr=(), sync=True):
        self.ops.append((q, fn, rd, wr, sync))

    def dma(self, q, out, in_, rd=(), wr=()):
        self.ops.append((q, lambda q=q, out=out, in_=in_: q.stream.dma_start(out, in_),
                         rd, wr, True))


def emit(ser, *plans):
    its = [list(p.ops) for p in plans]
    pos = [0] * len(its)
    left = sum(len(x) for x in its)
    while left:
        for j, ops in enumerate(its):
            if pos[j] < len(ops):
                q, fn, rd, wr, sync = ops[pos[j]]
                ser.do(q, fn, rd=rd, wr=wr, sync=sync)
                pos[j] += 1
                left -= 1


def build(nc, S=2048, NSEQ=2, taps=()):
    NT = S // 128
    NQST = S // 512
    NTT = NSEQ * NT
    GSZ = 4 if NTT % 4 == 0 else NTT
    NGRP = NTT // GSZ
    NBT = NT + 3
    ident_np, s33_np, W = build_consts(S)

    x_d = nc.dram_tensor("x", [NSEQ, S, D], F32, kind="ExternalInput").ap()
    wqk_d = nc.dram_tensor("wqk", [D, E], F32, kind="ExternalInput").ap()
    bqk_d = nc.dram_tensor("bqk", [E, 1], F32, kind="ExternalInput").ap()
    gam_d = nc.dram_tensor("gam", [2, E], F32, kind="ExternalInput").ap()
    bet_d = nc.dram_tensor("bet", [2, E], F32, kind="ExternalInput").ap()
    wv_d = nc.dram_tensor("wv", [D, D], F32, kind="ExternalInput").ap()
    bv_d = nc.dram_tensor("bv", [1, D], F32, kind="ExternalInput").ap()
    rel_d = nc.dram_tensor("rel", [32, 1], F32, kind="ExternalInput").ap()
    wg_d = nc.dram_tensor("wg", [H, D], F32, kind="ExternalInput").ap()
    wu_d = nc.dram_tensor("wu", [H, D], F32, kind="ExternalInput").ap()
    wd_d = nc.dram_tensor("wd", [D, H], F32, kind="ExternalInput").ap()
    out_d = nc.dram_tensor("out", [NSEQ, S, D], F32, kind="ExternalOutput").ap()

    ext2_d = nc.dram_tensor("ext2", [128, W], F32, kind="Internal").ap()
    x2s_d = nc.dram_tensor("x2s", [NSEQ, S, D], F32, kind="Internal").ap()
    wgqT_d = nc.dram_tensor("wgqT", [D, H], BF16, kind="Internal").ap()
    wuqT_d = nc.dram_tensor("wuqT", [D, H], BF16, kind="Internal").ap()
    wdqT_d = nc.dram_tensor("wdqT", [H, D], BF16, kind="Internal").ap()

    tap_d = {}
    for name, shape, dt in taps:
        tap_d[name] = nc.dram_tensor("tap_" + name, shape, dt,
                                     kind="ExternalOutput").ap()

    ident_d = nc.inline_tensor(ident_np, name="ident").ap()
    s33_d = nc.inline_tensor(s33_np, name="s33").ap()

    ser = Ser(nc)
    T, V, Sc, G = ser.T, ser.V, ser.S, ser.G
    DSY, DACT, DGP = ser.DSY, ser.DACT, ser.DGP

    stack = contextlib.ExitStack()
    PA = stack.enter_context(nc.psum_tensor("PA", [128, 512], F32)).ap()
    PB = stack.enter_context(nc.psum_tensor("PB", [128, 512], F32)).ap()
    PC = stack.enter_context(nc.psum_tensor("PC", [128, 512], F32)).ap()
    PD = stack.enter_context(nc.psum_tensor("PD", [128, 512], F32)).ap()
    PE = stack.enter_context(nc.psum_tensor("PE", [128, 512], F32)).ap()
    PF = stack.enter_context(nc.psum_tensor("PF", [128, 512], F32)).ap()
    PT0 = stack.enter_context(nc.psum_tensor("PT0", [128, 1024], BF16)).ap()
    PT1 = stack.enter_context(nc.psum_tensor("PT1", [128, 1024], BF16)).ap()
    PTs = [PT0, PT1]
    BKS = [(PA, "PA", PB, "PB"), (PC, "PC", PD, "PD"), (PE, "PE", PF, "PF")]

    def sb(name, shape, dt):
        return stack.enter_context(nc.sbuf_tensor(name, shape, dt)).ap()

    ident_s = sb("ident_s", [128, 128], BF16)
    bqk_s = sb("bqk_s", [128, 1], F32)
    g0s = sb("g0s", [128, 1], F32)
    b0s = sb("b0s", [128, 1], F32)
    g1s = sb("g1s", [128, 1], F32)
    b1s = sb("b1s", [128, 1], F32)
    gb_tmp = sb("gb_tmp", [128, 4], F32)
    strip = sb("strip", [128, 6 * NT * NSEQ], F32)

    def strip_col(s, which, tt):
        return (s * 6 + which) * NT + tt

    ser.dma(DSY, ident_s[:], ident_d[:], wr=["params"])
    ser.dma(DSY, bqk_s[:], bqk_d[:], wr=["params"])
    ser.dma(DSY, gb_tmp[:, 0:1], gam_d[0].rearrange("(n o) -> n o", o=1), wr=["params"])
    ser.dma(DSY, gb_tmp[:, 1:2], bet_d[0].rearrange("(n o) -> n o", o=1), wr=["params"])
    ser.dma(DSY, gb_tmp[:, 2:3], gam_d[1].rearrange("(n o) -> n o", o=1), wr=["params"])
    ser.dma(DSY, gb_tmp[:, 3:4], bet_d[1].rearrange("(n o) -> n o", o=1), wr=["params"])
    inv_s = c32(1.0 / S)
    ser.do(V, lambda: V.stream.tensor_scalar(g0s[:], gb_tmp[:, 0:1], inv_s, None, OP.mult), rd=["params"], wr=["params2"])
    ser.do(V, lambda: V.stream.tensor_scalar(b0s[:], gb_tmp[:, 1:2], inv_s, None, OP.mult), rd=["params"], wr=["params2"])
    ser.do(V, lambda: V.stream.tensor_copy(g1s[:], gb_tmp[:, 2:3]), rd=["params"], wr=["params2"])
    ser.do(V, lambda: V.stream.tensor_copy(b1s[:], gb_tmp[:, 3:4]), rd=["params"], wr=["params2"])

    def quant_tile(em, x_key, x_ap, out_key, out_ap, F, scr, pre):
        NB = F // 16
        bm, s_, rc, rb2 = scr["bm"], scr["s_"], scr["rc"], scr["rb2"]
        x2 = scr["x2"]
        z = x2
        c0, c1, c2 = scr["c0"], scr["c1"], scr["c2"]
        kbm, ks, krc, krb = pre + "bm", pre + "s_", pre + "rc", pre + "rb2"
        kx2 = pre + "x2"
        kz = kx2
        kc0, kc1, kc2 = pre + "c0", pre + "c1", pre + "c2"
        x3 = x_ap.rearrange("p (b k) -> p b k", k=16)
        em.do(Sc, lambda: Sc.stream.activation(x2[:, :F], x_ap, AF.Square),
              rd=[x_key], wr=[kx2])
        em.do(V, lambda: V.stream.tensor_reduce(bm[:, :NB], x3, AX.X, OP.max,
                                                apply_absolute_value=True),
              rd=[x_key], wr=[kbm])
        em.do(V, lambda: V.stream.reciprocal(rc[:, :NB], bm[:, :NB]),
              rd=[kbm], wr=[krc])
        em.do(V, lambda: V.stream.tensor_scalar(rc[:, :NB], rc[:, :NB], 6.0, c32(1e12), OP.mult, OP.min),
              rd=[krc], wr=[krc])
        em.do(V, lambda: V.stream.tensor_mul(rb2[:, :NB], rc[:, :NB], rc[:, :NB]),
              rd=[krc], wr=[krb])
        em.do(V, lambda: V.stream.reciprocal(s_[:, :NB], rc[:, :NB]),
              rd=[krc], wr=[ks])
        rb2b = rb2[:, :NB].rearrange("p (b o) -> p b o", o=1).broadcast_to([128, NB, 16])
        em.do(V, lambda: V.stream.tensor_tensor(
            z[:, :F].rearrange("p (b k) -> p b k", k=16),
            x2[:, :F].rearrange("p (b k) -> p b k", k=16), rb2b, OP.mult),
            rd=[kx2, krb], wr=[kz])
        zf = z[:, :F]

        def step(dst, dk, thr):
            em.do(Sc, lambda: Sc.stream.activation(
                dst, zf, AF.Sigmoid, bias=np.float32(-thr * 1e30), scale=BIG),
                rd=[kz], wr=[dk])

        c0f, c1f, c2f = c0[:, :F], c1[:, :F], c2[:, :F]
        csg = scr["sg"][:, :F]
        ksg = pre + "sg"
        em.do(Sc, lambda: Sc.stream.activation(csg, x_ap, AF.Sign),
              rd=[x_key], wr=[ksg])
        step(c0f, kc0, QTHR[0])
        step(c1f, kc1, QTHR[1])
        em.do(V, lambda: V.stream.tensor_add(c0f, c0f, c1f), rd=[kc0, kc1], wr=[kc0])
        step(c1f, kc1, QTHR[2])
        step(c2f, kc2, QTHR[3])
        em.do(V, lambda: V.stream.tensor_add(c1f, c1f, c2f), rd=[kc1, kc2], wr=[kc1])
        em.do(V, lambda: V.stream.tensor_add(c0f, c0f, c1f), rd=[kc0, kc1], wr=[kc0])
        step(c1f, kc1, QTHR[4])
        step(c2f, kc2, QTHR[5])
        em.do(V, lambda: V.stream.tensor_add(c1f, c1f, c2f), rd=[kc1, kc2], wr=[kc1])
        step(c2f, kc2, QTHR[6])
        em.do(V, lambda: V.stream.scalar_tensor_tensor(
            c1f, c2f, 2.0, c1f, OP.mult, OP.add), rd=[kc1, kc2], wr=[kc1])
        em.do(V, lambda: V.stream.scalar_tensor_tensor(
            c0f, c1f, 2.0, c0f, OP.mult, OP.add), rd=[kc0, kc1], wr=[kc0])
        em.do(V, lambda: V.stream.tensor_mul(c2f, c0f, csg), rd=[kc0, ksg], wr=[kc2])
        s_b = s_[:, :NB].rearrange("p (b o) -> p b o", o=1).broadcast_to([128, NB, 16])
        em.do(V, lambda: V.stream.scalar_tensor_tensor(
            out_ap.rearrange("p (b k) -> p b k", k=16),
            c2f.rearrange("p (b k) -> p b k", k=16), 0.5, s_b,
            OP.mult, OP.mult), rd=[kc2, ks], wr=[out_key])

    with contextlib.ExitStack() as ph:
        def psb(name, shape, dt):
            return ph.enter_context(nc.sbuf_tensor(name, shape, dt)).ap()

        rel_s = psb("rel_s", [33, 128], F32)
        s33_s = psb("s33_s", [33, W], F32)
        ext_rep = psb("ext_rep", [128, W], F32)
        rel_col = psb("rel_col", [32, 1], F32)
        ser.dma(DSY, rel_col[:], rel_d[:], wr=["rel"])
        ser.dma(DSY, s33_s[:], s33_d[:], wr=["s33"])
        ser.do(V, lambda: V.stream.tensor_copy(rel_s[0:32, :],
                                               rel_col.broadcast_to([32, 128])),
               rd=["rel"], wr=["rel_s"])
        ser.do(V, lambda: V.stream.memset(rel_s[32:33, :], 1.0), wr=["rel_s"])
        nch = (W + 511) // 512
        for c in range(nch):
            cw = min(512, W - c * 512)
            ser.do(T, lambda c=c, cw=cw: T.stream.matmul(
                PA[:, 0:cw], rel_s[:, :], s33_s[:, c * 512:c * 512 + cw],
                start=True, stop=True), rd=["rel_s", "s33"], wr=["PA"])
            ser.do(V, lambda c=c, cw=cw: V.stream.tensor_copy(
                ext_rep[:, c * 512:c * 512 + cw], PA[:, 0:cw]),
                rd=["PA"], wr=["ext_rep"])
        ser.dma(DSY, ext2_d[:], ext_rep[:], rd=["ext_rep"], wr=["d:ext2"])
    ser.fence()

    with contextlib.ExitStack() as ph:
        def psb(name, shape, dt):
            return ph.enter_context(nc.sbuf_tensor(name, shape, dt)).ap()

        q_s = psb("q_s", [128, S], BF16)
        k_s = psb("k_s", [128, S], BF16)
        biasT = psb("biasT", [128, NBT * 128], F32)
        for ot in range(NBT):
            bsrc = AP(ext2_d.tensor, 128 + ot * 128, [[W - 1, 128], [1, 128]])
            ser.dma(DSY, biasT[:, ot * 128:(ot + 1) * 128], bsrc,
                    rd=["d:ext2"], wr=["biasT"])

        wqk_s = psb("wqk_s", [128, NCF * E], BF16)
        wv_s = psb("wv_s", [128, NCF * D], BF16)
        bv_rep = psb("bv_rep", [128, D], F32)
        ser.dma(DGP, wqk_s.rearrange("p (c e) -> p c e", c=NCF),
                wqk_d.rearrange("(c p) e -> p c e", p=128), wr=["wqk_s"])
        ser.dma(DGP, wv_s.rearrange("p (c n) -> p c n", c=NCF),
                wv_d.rearrange("(c p) n -> p c n", p=128), wr=["wv_s"])
        bv_row = psb("bv_row", [1, D], F32)
        ones_r = psb("ones_r", [1, 128], F32)
        ser.dma(DSY, bv_row[:], bv_d[:], wr=["bv_row"])
        ser.do(V, lambda: V.stream.memset(ones_r[:], 1.0), wr=["ones_r"])
        for hh in range(2):
            ser.do(T, lambda hh=hh: T.stream.matmul(
                PA[:, :], ones_r[:, :], bv_row[:, hh * 512:(hh + 1) * 512],
                start=True, stop=True), rd=["ones_r", "bv_row"], wr=["PA"])
            ser.do(V, lambda hh=hh: V.stream.tensor_copy(
                bv_rep[:, hh * 512:(hh + 1) * 512], PA[:, :]),
                rd=["PA"], wr=["bv_rep"])

        xt = [psb("xt0", [128, D], F32), psb("xt1", [128, D], F32)]
        xsq = [psb("xsq0", [128, D], F32), psb("xsq1", [128, D], F32)]
        ln1b = [psb("ln1b0", [128, D], BF16), psb("ln1b1", [128, D], BF16)]
        ln1T = psb("ln1T", [128, NCF * S], BF16)
        v_s = psb("v_s", [128, NT * D], BF16)
        attn_b = psb("attn_b", [128, NT * 512], BF16)
        simb = [psb("simb0", [128, 512], F32), psb("simb1", [128, 512], F32)]
        erf_s = [psb("erf0", [128, 512], F32), psb("erf1", [128, 512], F32)]
        uu = [psb("uu0", [128, 512], F32), psb("uu1", [128, 512], F32)]
        sg_s = [psb("sg0", [128, 512], F32), psb("sg1", [128, 512], F32)]
        x2t = [psb("x2t0", [128, D], F32), psb("x2t1", [128, D], F32)]

        for s in range(NSEQ):
            c0_, c1_ = strip_col(s, 0, 0), strip_col(s, 1, 0)

            def stats_plan(tt, b):
                p = Plan()
                xk, xa = "xt%d" % b, xt[b]
                qk_, qa = "xsq%d" % b, xsq[b]
                p.dma(DSY, xa[:], x_d[s, tt * 128:(tt + 1) * 128, :], wr=[xk])
                p.do(V, lambda: V.stream.reduce_sum(
                    strip[:, strip_col(s, 0, tt):strip_col(s, 0, tt) + 1],
                    xa[:], AX.X), rd=[xk], wr=["strip"])
                p.do(Sc, lambda: Sc.stream.activation(qa[:], xa[:], AF.Square),
                     rd=[xk], wr=[qk_])
                p.do(V, lambda: V.stream.reduce_sum(
                    strip[:, strip_col(s, 1, tt):strip_col(s, 1, tt) + 1],
                    qa[:], AX.X), rd=[qk_], wr=["strip"])
                return p

            for tt in range(0, NT, 2):
                emit(ser, stats_plan(tt, 0), stats_plan(tt + 1, 1))

            m1 = strip[:, c0_:c0_ + NT]
            v1 = strip[:, c1_:c1_ + NT]
            ser.do(V, lambda m1=m1: V.stream.tensor_scalar(m1, m1, c32(1.0 / D), None, OP.mult), rd=["strip"], wr=["strip"])
            ser.do(V, lambda v1=v1: V.stream.tensor_scalar(v1, v1, c32(1.0 / D), None, OP.mult), rd=["strip"], wr=["strip"])
            ser.do(V, lambda m1=m1: V.stream.tensor_tensor(xsq[0][:, 0:NT], m1, m1, OP.mult), rd=["strip"], wr=["xsq0"])
            ser.do(V, lambda v1=v1: V.stream.tensor_sub(v1, v1, xsq[0][:, 0:NT]), rd=["strip", "xsq0"], wr=["strip"])
            ser.do(V, lambda v1=v1: V.stream.tensor_scalar(v1, v1, c32(1e-5), None, OP.add), rd=["strip"], wr=["strip"])
            ser.do(Sc, lambda v1=v1: Sc.stream.activation(v1, v1, AF.Sqrt), rd=["strip"], wr=["strip"])
            ser.do(V, lambda v1=v1: V.stream.reciprocal(v1, v1), rd=["strip"], wr=["strip"])
            ser.do(V, lambda m1=m1, v1=v1: V.stream.tensor_mul(m1, m1, v1), rd=["strip"], wr=["strip"])
            ser.do(V, lambda m1=m1: V.stream.tensor_scalar(m1, m1, -1.0, None, OP.mult), rd=["strip"], wr=["strip"])

            def norm_plan(tt, b):
                p = Plan()
                xk, xa = "xt%d" % b, xt[b]
                lk, la = "ln1b%d" % b, ln1b[b]
                ptk = "PT%d" % b
                p.dma(DSY, xa[:], x_d[s, tt * 128:(tt + 1) * 128, :], wr=[xk])
                p.do(V, lambda: V.stream.tensor_scalar(
                    la[:], xa[:], strip[:, c1_ + tt:c1_ + tt + 1],
                    strip[:, c0_ + tt:c0_ + tt + 1], OP.mult, OP.add),
                    rd=[xk, "strip"], wr=[lk])
                for c in range(NCF):
                    p.do(T, lambda c=c: T.stream.transpose(
                        PTs[b][:, c * 128:(c + 1) * 128],
                        la[:, c * 128:(c + 1) * 128], ident_s[:]),
                        rd=[lk, "params"], wr=[ptk], sync=(c == NCF - 1))
                p.do(V, lambda: V.stream.tensor_copy(
                    ln1T.rearrange("p (c t) -> p c t", c=NCF)[:, :, tt * 128:(tt + 1) * 128],
                    PTs[b].rearrange("p (c t) -> p c t", c=NCF)),
                    rd=[ptk], wr=["ln1T"])
                return p

            for tt in range(0, NT, 2):
                emit(ser, norm_plan(tt, 0), norm_plan(tt + 1, 1))

            def qk_plan(tb, b):
                p = Plan()
                PP, pk, _, _ = BKS[b]
                uk, ua = "uu%d" % b, uu[b]
                gk, ga = "sg%d" % b, sg_s[b]
                for c in range(NCF):
                    p.do(T, lambda c=c: T.stream.matmul(
                        PP[:, :], wqk_s[:, c * E:(c + 1) * E],
                        ln1T[:, c * S + tb * 512: c * S + (tb + 1) * 512],
                        start=(c == 0), stop=(c == NCF - 1)),
                        rd=["wqk_s", "ln1T"], wr=[pk], sync=(c == NCF - 1))
                p.do(V, lambda: V.stream.tensor_scalar(ua[:], PP[:, :], bqk_s[:], None, OP.add),
                     rd=[pk, "params"], wr=[uk])
                p.do(Sc, lambda: Sc.stream.activation(ga[:], ua[:], AF.Sigmoid),
                     rd=[uk], wr=[gk])
                p.do(V, lambda: V.stream.tensor_mul(ua[:], ua[:], ga[:]),
                     rd=[uk, gk], wr=[uk])
                p.do(V, lambda: V.stream.tensor_scalar(
                    q_s[:, tb * 512:(tb + 1) * 512], ua[:], g0s[:], b0s[:],
                    OP.mult, OP.add), rd=[uk, "params2"], wr=["q_s"])
                p.do(V, lambda: V.stream.tensor_scalar(
                    k_s[:, tb * 512:(tb + 1) * 512], ua[:], g1s[:], b1s[:],
                    OP.mult, OP.add), rd=[uk, "params2"], wr=["k_s"])
                return p

            tbs = list(range(S // 512))
            for ii in range(0, len(tbs), 2):
                plans = [qk_plan(tbs[ii], 0)]
                if ii + 1 < len(tbs):
                    plans.append(qk_plan(tbs[ii + 1], 1))
                emit(ser, *plans)

            def v_plan(tt, b):
                p = Plan()
                P0, k0, P1, k1 = BKS[b]
                uk, ua = "uu%d" % b, uu[b]
                gk, ga = "sg%d" % b, sg_s[b]
                for hh in range(2):
                    PP, pk = (P0, k0) if hh == 0 else (P1, k1)
                    for c in range(NCF):
                        p.do(T, lambda c=c, PP=PP, hh=hh: T.stream.matmul(
                            PP[:, :],
                            ln1T[:, c * S + tt * 128: c * S + (tt + 1) * 128],
                            wv_s[:, c * D + hh * 512: c * D + (hh + 1) * 512],
                            start=(c == 0), stop=(c == NCF - 1)),
                            rd=["ln1T", "wv_s"], wr=[pk], sync=(c == NCF - 1))
                    p.do(V, lambda PP=PP, pk=pk, hh=hh: V.stream.tensor_add(
                        ua[:], PP[:, :], bv_rep[:, hh * 512:(hh + 1) * 512]),
                        rd=[pk, "bv_rep"], wr=[uk])
                    p.do(Sc, lambda: Sc.stream.activation(ga[:], ua[:], AF.Sigmoid),
                         rd=[uk], wr=[gk])
                    p.do(V, lambda hh=hh: V.stream.tensor_mul(
                        v_s[:, tt * D + hh * 512: tt * D + (hh + 1) * 512],
                        ua[:], ga[:]), rd=[uk, gk], wr=["v_s"])
                return p

            for tt in range(0, NT, 2):
                emit(ser, v_plan(tt, 0), v_plan(tt + 1, 1))

            for qst in range(NQST):
                kmax = qst * 4 + 4

                def sim_plan(kb, b):
                    p = Plan()
                    PP, pk, _, _ = BKS[b]
                    sk, sa = "simb%d" % b, simb[b]
                    ek, ea = "erf%d" % b, erf_s[b]
                    p.do(T, lambda: T.stream.matmul(
                        PP[:, :], k_s[:, kb * 128:(kb + 1) * 128],
                        q_s[:, qst * 512:(qst + 1) * 512],
                        start=True, stop=True), rd=["k_s", "q_s"], wr=[pk])
                    t0 = qst * 4 - kb + 3
                    p.do(V, lambda: V.stream.tensor_add(
                        sa[:], PP[:, :], biasT[:, t0 * 128:t0 * 128 + 512]),
                        rd=[pk, "biasT"], wr=[sk])
                    p.do(Sc, lambda: Sc.stream.activation(
                        ea[:], sa[:], AF.Erf, bias=ERF_BIAS, scale=ERF_SCALE),
                        rd=[sk], wr=[ek])
                    p.do(V, lambda: V.stream.tensor_scalar(
                        attn_b[:, kb * 512:(kb + 1) * 512], ea[:],
                        0.5, 0.5, OP.mult, OP.add), rd=[ek], wr=["attn_b"])
                    return p

                for kb in range(0, kmax, 2):
                    plans = [sim_plan(kb, 0)]
                    if kb + 1 < kmax:
                        plans.append(sim_plan(kb + 1, 1))
                    emit(ser, *plans)

                def av_plan(j, b):
                    p = Plan()
                    qt = qst * 4 + j
                    P0, k0, P1, k1 = BKS[b]
                    xk, xa = "xt%d" % b, xt[b]
                    x2k, x2a = "x2t%d" % b, x2t[b]
                    qk_, qa = "xsq%d" % b, xsq[b]
                    p.dma(DSY, xa[:], x_d[s, qt * 128:(qt + 1) * 128, :], wr=[xk])
                    for kb in range(qt + 1):
                        for hh in range(2):
                            PP, pk = (P0, k0) if hh == 0 else (P1, k1)
                            p.do(T, lambda kb=kb, PP=PP, hh=hh: T.stream.matmul(
                                PP[:, :],
                                attn_b[:, kb * 512 + j * 128: kb * 512 + (j + 1) * 128],
                                v_s[:, kb * D + hh * 512: kb * D + (hh + 1) * 512],
                                start=(kb == 0), stop=(kb == qt)),
                                rd=["attn_b", "v_s"], wr=[pk], sync=(kb == qt))
                    p.do(V, lambda: V.stream.tensor_add(x2a[:, 0:512], P0[:, :], xa[:, 0:512]),
                         rd=[k0, xk], wr=[x2k])
                    p.do(V, lambda: V.stream.tensor_add(x2a[:, 512:1024], P1[:, :], xa[:, 512:1024]),
                         rd=[k1, xk], wr=[x2k])
                    p.do(V, lambda qt=qt: V.stream.reduce_sum(
                        strip[:, strip_col(s, 2, qt):strip_col(s, 2, qt) + 1],
                        x2a[:], AX.X), rd=[x2k], wr=["strip"])
                    p.do(Sc, lambda: Sc.stream.activation(qa[:], x2a[:], AF.Square),
                         rd=[x2k], wr=[qk_])
                    p.do(V, lambda qt=qt: V.stream.reduce_sum(
                        strip[:, strip_col(s, 3, qt):strip_col(s, 3, qt) + 1],
                        qa[:], AX.X), rd=[qk_], wr=["strip"])
                    p.dma(DSY, x2s_d[s, qt * 128:(qt + 1) * 128, :], x2a[:],
                          rd=[x2k], wr=["d:x2s"])
                    return p

                for j in range(0, 4, 2):
                    emit(ser, av_plan(j, 0), av_plan(j + 1, 1))

            c2_, c3_ = strip_col(s, 2, 0), strip_col(s, 3, 0)
            m2 = strip[:, c2_:c2_ + NT]
            v2 = strip[:, c3_:c3_ + NT]
            ser.do(V, lambda m2=m2: V.stream.tensor_scalar(m2, m2, c32(1.0 / D), None, OP.mult), rd=["strip"], wr=["strip"])
            ser.do(V, lambda v2=v2: V.stream.tensor_scalar(v2, v2, c32(1.0 / D), None, OP.mult), rd=["strip"], wr=["strip"])
            ser.do(V, lambda m2=m2: V.stream.tensor_tensor(xsq[0][:, 0:NT], m2, m2, OP.mult), rd=["strip"], wr=["xsq0"])
            ser.do(V, lambda v2=v2: V.stream.tensor_sub(v2, v2, xsq[0][:, 0:NT]), rd=["strip", "xsq0"], wr=["strip"])
            ser.do(V, lambda v2=v2: V.stream.tensor_scalar(v2, v2, c32(1e-5), None, OP.add), rd=["strip"], wr=["strip"])
            ser.do(Sc, lambda v2=v2: Sc.stream.activation(v2, v2, AF.Sqrt), rd=["strip"], wr=["strip"])
            ser.do(V, lambda v2=v2: V.stream.reciprocal(v2, v2), rd=["strip"], wr=["strip"])
            ser.do(V, lambda m2=m2, v2=v2: V.stream.tensor_mul(m2, m2, v2), rd=["strip"], wr=["strip"])
            ser.do(V, lambda m2=m2: V.stream.tensor_scalar(m2, m2, -1.0, None, OP.mult), rd=["strip"], wr=["strip"])

        if "v" in tap_d:
            ser.dma(DSY, tap_d["v"][:], v_s[:], rd=["v_s"])
        if "q" in tap_d:
            ser.dma(DSY, tap_d["q"][:], q_s[:], rd=["q_s"])
        if "k" in tap_d:
            ser.dma(DSY, tap_d["k"][:], k_s[:], rd=["k_s"])
        if "attnb" in tap_d:
            ser.dma(DSY, tap_d["attnb"][:], attn_b[:], rd=["attn_b"])
    ser.fence()

    with contextlib.ExitStack() as ph:
        def psb(name, shape, dt):
            return ph.enter_context(nc.sbuf_tensor(name, shape, dt)).ap()

        wt = [psb("wt0", [128, H], F32), psb("wt1", [128, H], F32)]
        wq = [psb("wq0", [128, H], BF16), psb("wq1", [128, H], BF16)]
        wtq = [psb("wtq0", [128, NCH * 128], BF16), psb("wtq1", [128, NCH * 128], BF16)]
        scrw = []
        for b in range(2):
            scrw.append({
                "bm": psb("w%d_bm" % b, [128, H // 16], F32),
                "s_": psb("w%d_s_" % b, [128, H // 16], F32),
                "rc": psb("w%d_rc" % b, [128, H // 16], F32),
                "rb2": psb("w%d_rb2" % b, [128, H // 16], F32),
                "x2": psb("w%d_x2" % b, [128, H], F32),
                 "c0": psb("w%d_c0" % b, [128, H], BF16),
                "c1": psb("w%d_c1" % b, [128, H], BF16),
                "c2": psb("w%d_c2" % b, [128, H], BF16),
                "sg": psb("w%d_sg" % b, [128, H], BF16)})

        jobs = []
        for w_in, w_out, wok, nt_, Fw in ((wg_d, wgqT_d, "d:wgqT", NCH, D),
                                          (wu_d, wuqT_d, "d:wuqT", NCH, D),
                                          (wd_d, wdqT_d, "d:wdqT", NCF, H)):
            t = 0
            while t < nt_:
                np_ = 2 if (Fw == D and t + 1 < nt_) else 1
                jobs.append((w_in, w_out, wok, t, Fw, np_))
                t += np_

        def w_plan(job, b):
            w_in, w_out, wok, t, Fw, np_ = job
            Ft = Fw * np_
            nch_t = Fw // 128
            p = Plan()
            wk, wa = "wt%d" % b, wt[b]
            qk_, qa = "wq%d" % b, wq[b]
            tk, ta = "wtq%d" % b, wtq[b]
            for u in range(np_):
                p.dma(DSY, wa[:, u * Fw:(u + 1) * Fw],
                      w_in[(t + u) * 128:(t + u + 1) * 128, :], wr=[wk])
            quant_tile(p, wk, wa[:, :Ft], qk_, qa[:, :Ft], Ft, scrw[b], pre="w%d" % b)
            nch = Ft // 128
            for cb in range(0, nch, 8):
                cn = min(8, nch - cb)
                for c in range(cn):
                    p.do(T, lambda c=c, cb=cb: T.stream.transpose(
                        PTs[b][:, c * 128:(c + 1) * 128],
                        qa[:, (cb + c) * 128:(cb + c + 1) * 128], ident_s[:]),
                        rd=[qk_, "params"], wr=["PT%d" % b], sync=(c == cn - 1))
                p.do(V, lambda cb=cb, cn=cn: V.stream.tensor_copy(
                    ta[:, cb * 128:(cb + cn) * 128], PTs[b][:, 0:cn * 128]),
                    rd=["PT%d" % b], wr=[tk])
            for u in range(np_):
                p.dma(DSY, w_out.rearrange("(c p) n -> p c n", p=128)[:, :, (t + u) * 128:(t + u + 1) * 128],
                      ta.rearrange("p (c t) -> p c t", t=128)[:, u * nch_t:(u + 1) * nch_t, :],
                      rd=[tk], wr=[wok])
            return p

        for ii in range(0, len(jobs), 2):
            plans = [w_plan(jobs[ii], 0)]
            if ii + 1 < len(jobs):
                plans.append(w_plan(jobs[ii + 1], 1))
            emit(ser, *plans)
    ser.fence()

    SL = [512] * (H // 512) + ([H % 512] if H % 512 else [])
    with contextlib.ExitStack() as ph:
        def psb(name, shape, dt):
            return ph.enter_context(nc.sbuf_tensor(name, shape, dt)).ap()

        NW3 = 3
        x2g = psb("x2g", [128, GSZ * D], F32)
        ln2f = [psb("ln2f%d" % b, [128, D], F32) for b in range(NW3)]
        ln2qb = [psb("ln2qb%d" % b, [128, D], BF16) for b in range(NW3)]
        ln2qT = psb("ln2qT", [128, NCF * GSZ * 128], BF16)
        wgs = [psb("wgs0", [128, NCF * 512], BF16), psb("wgs1", [128, NCF * 512], BF16)]
        wus = [psb("wus0", [128, NCF * 512], BF16), psb("wus1", [128, NCF * 512], BF16)]
        hqT = psb("hqT", [128, NCH * GSZ * 128], BF16)
        wdq_s = psb("wdq_s", [128, NCH * D], BF16)
        hfull = [psb("hfull%d" % b, [128, 1024], F32) for b in range(NW3)]
        sgf = [psb("sgf%d" % b, [128, 512], F32) for b in range(NW3)]
        otile = [psb("otile%d" % b, [128, D], F32) for b in range(2)]
        scr3 = []
        for b in range(NW3):
            scr3.append({
                "bm": psb("p3%d_bm" % b, [128, D // 16], F32),
                "s_": psb("p3%d_s_" % b, [128, D // 16], F32),
                "rc": psb("p3%d_rc" % b, [128, D // 16], F32),
                "rb2": psb("p3%d_rb2" % b, [128, D // 16], F32),
                "x2": psb("p3%d_x2" % b, [128, D], F32),
                "c0": psb("p3%d_c0" % b, [128, D], BF16),
                "c1": psb("p3%d_c1" % b, [128, D], BF16),
                "c2": psb("p3%d_c2" % b, [128, D], BF16),
                "sg": psb("p3%d_sg" % b, [128, D], BF16)})

        ser.dma(DSY, wdq_s.rearrange("p (c n) -> p c n", c=NCH),
                wdqT_d.rearrange("(c p) n -> p c n", p=128),
                rd=["d:wdqT"], wr=["wdq_s"])

        def load_slab(sl_i):
            off = 512 * sl_i
            slw = SL[sl_i]
            b = sl_i % 2
            ser.dma(DSY, wgs[b].rearrange("p (c n) -> p c n", c=NCF)[:, :, 0:slw],
                    wgqT_d.rearrange("(c p) n -> p c n", p=128)[:, :, off:off + slw],
                    rd=["d:wgqT"], wr=["wgs%d" % b])
            ser.dma(DSY, wus[b].rearrange("p (c n) -> p c n", c=NCF)[:, :, 0:slw],
                    wuqT_d.rearrange("(c p) n -> p c n", p=128)[:, :, off:off + slw],
                    rd=["d:wuqT"], wr=["wus%d" % b])

        for grp in range(NGRP):
            load_slab(0)

            ptc = [0]

            def ln2_plan(i, b):
                p = Plan()
                gt = grp * GSZ + i
                s, qt = divmod(gt, NT)
                xgk = "x2g%d" % i
                lk, la = "ln2f%d" % b, ln2f[b]
                qk_, qa = "ln2qb%d" % b, ln2qb[b]
                p.dma(DSY, x2g[:, i * D:(i + 1) * D],
                      x2s_d[s, qt * 128:(qt + 1) * 128, :],
                      rd=["d:x2s"], wr=[xgk])
                p.do(V, lambda: V.stream.tensor_scalar(
                    la[:], x2g[:, i * D:(i + 1) * D],
                    strip[:, strip_col(s, 3, qt):strip_col(s, 3, qt) + 1],
                    strip[:, strip_col(s, 2, qt):strip_col(s, 2, qt) + 1],
                    OP.mult, OP.add), rd=[xgk, "strip"], wr=[lk])
                quant_tile(p, lk, la[:], qk_, qa[:], D, scr3[b], pre="p3%d" % b)
                pt = ptc[0] % 2
                ptc[0] += 1
                p2 = Plan()
                for c in range(NCF):
                    p2.do(T, lambda c=c, pt=pt: T.stream.transpose(
                        PTs[pt][:, c * 128:(c + 1) * 128],
                        qa[:, c * 128:(c + 1) * 128], ident_s[:]),
                        rd=[qk_, "params"], wr=["PT%d" % pt], sync=(c == NCF - 1))
                p2.do(V, lambda pt=pt: V.stream.tensor_copy(
                    ln2qT.rearrange("p (c g) -> p c g", c=NCF)[:, :, i * 128:(i + 1) * 128],
                    PTs[pt].rearrange("p (c t) -> p c t", c=NCF)),
                    rd=["PT%d" % pt], wr=["ln2qT%d" % i])
                return p, p2

            lp = [ln2_plan(i, i % 3) for i in range(GSZ)]
            comp = [a for a, _ in lp]
            tps = [b_ for _, b_ in lp]
            k = 0
            while k + 3 <= len(comp):
                emit(ser, comp[k], comp[k + 1], comp[k + 2])
                emit(ser, tps[k], tps[k + 1])
                emit(ser, tps[k + 2])
                k += 3
            if k < len(comp):
                emit(ser, *comp[k:])
                for j in range(k, len(comp)):
                    emit(ser, tps[j])

            hb3 = [0]
            cur = []
            curt = []
            emitted_c = [0]
            pending_loads = []

            def load_plan(sl_i):
                off = 512 * sl_i
                slw = SL[sl_i]
                b = sl_i % 2
                p = Plan()
                p.dma(DSY, wgs[b].rearrange("p (c n) -> p c n", c=NCF)[:, :, 0:slw],
                      wgqT_d.rearrange("(c p) n -> p c n", p=128)[:, :, off:off + slw],
                      rd=["d:wgqT"], wr=["wgs%d" % b])
                p.dma(DSY, wus[b].rearrange("p (c n) -> p c n", c=NCF)[:, :, 0:slw],
                      wuqT_d.rearrange("(c p) n -> p c n", p=128)[:, :, off:off + slw],
                      rd=["d:wuqT"], wr=["wus%d" % b])
                return p

            def drain_loads():
                while pending_loads and pending_loads[0][0] <= emitted_c[0]:
                    emit(ser, pending_loads.pop(0)[1])

            def flush():
                if not cur:
                    return
                emit(ser, *cur)
                emitted_c[0] += len(cur)
                del cur[:]
                if len(curt) >= 2:
                    emit(ser, curt[0], curt[1])
                    del curt[:2]
                while curt:
                    emit(ser, curt.pop(0))
                drain_loads()

            for sl_i, slw in enumerate(SL):
                sb_ = sl_i % 2
                wgk, wuk = "wgs%d" % sb_, "wus%d" % sb_
                off = 512 * sl_i

                def h_plan(i0, b, sl_i=sl_i, slw=slw, sb_=sb_, wgk=wgk, wuk=wuk, off=off):
                    p = Plan()
                    P0, k0, P1, k1 = BKS[b]
                    gk, ga = "sgf%d" % b, sgf[b]
                    hk, ha = "hfull%d" % b, hfull[b]
                    qk_, qa = "ln2qb%d" % b, ln2qb[b]
                    for u in range(2):
                        i = i0 + u
                        uo = u * slw
                        for c in range(NCF):
                            p.do(T, lambda c=c, i=i: T.stream.matmul(
                                P0[:, 0:slw],
                                ln2qT[:, c * GSZ * 128 + i * 128: c * GSZ * 128 + (i + 1) * 128],
                                wgs[sb_][:, c * 512: c * 512 + slw],
                                start=(c == 0), stop=(c == NCF - 1)),
                                rd=["ln2qT%d" % i, wgk], wr=[k0], sync=(c == NCF - 1))
                            p.do(T, lambda c=c, i=i: T.stream.matmul(
                                P1[:, 0:slw],
                                ln2qT[:, c * GSZ * 128 + i * 128: c * GSZ * 128 + (i + 1) * 128],
                                wus[sb_][:, c * 512: c * 512 + slw],
                                start=(c == 0), stop=(c == NCF - 1)),
                                rd=["ln2qT%d" % i, wuk], wr=[k1], sync=(c == NCF - 1))
                        p.do(Sc, lambda: Sc.stream.activation(
                            ga[:, 0:slw], P0[:, 0:slw], AF.Sigmoid), rd=[k0], wr=[gk])
                        p.do(V, lambda: V.stream.tensor_mul(
                            ga[:, 0:slw], ga[:, 0:slw], P0[:, 0:slw]), rd=[gk, k0], wr=[gk])
                        p.do(V, lambda uo=uo: V.stream.tensor_mul(
                            ha[:, uo:uo + slw], ga[:, 0:slw], P1[:, 0:slw]), rd=[gk, k1], wr=[hk])
                    quant_tile(p, hk, ha[:, 0:2 * slw], qk_, qa[:, 0:2 * slw], 2 * slw,
                               scr3[b], pre="p3%d" % b)
                    pt = ptc[0] % 2
                    ptc[0] += 1
                    p2 = Plan()
                    for c in range(2 * slw // 128):
                        p2.do(T, lambda c=c, pt=pt: T.stream.transpose(
                            PTs[pt][:, c * 128:(c + 1) * 128],
                            qa[:, c * 128:(c + 1) * 128], ident_s[:]),
                            rd=[qk_, "params"], wr=["PT%d" % pt], sync=(c == 2 * slw // 128 - 1))
                    for u in range(2):
                        i = i0 + u
                        p2.do(V, lambda u=u, i=i, pt=pt: V.stream.tensor_copy(
                            hqT.rearrange("p (c g) -> p c g", c=NCH)[:, off // 128:(off + slw) // 128, i * 128:(i + 1) * 128],
                            PTs[pt].rearrange("p (c t) -> p c t", t=128)[:, u * (slw // 128):(u + 1) * (slw // 128), :]),
                            rd=["PT%d" % pt], wr=["hqT%d" % i])
                    return p, p2

                if sl_i + 1 < len(SL):
                    pending_loads.append((2 * sl_i, load_plan(sl_i + 1)))
                    drain_loads()
                for i0 in range(0, GSZ, 2):
                    pc_, pt_ = h_plan(i0, hb3[0] % NW3)
                    hb3[0] += 1
                    cur.append(pc_)
                    curt.append(pt_)
                    if len(cur) == 3:
                        flush()
            flush()
            def dn_plan(i, b):
                p = Plan()
                gt = grp * GSZ + i
                s, qt = divmod(gt, NT)
                P0, k0, P1, k1 = BKS[b]
                ok, oa = "otile%d" % b, otile[b]
                for hh in range(2):
                    PP, pk = (P0, k0) if hh == 0 else (P1, k1)
                    for hc in range(NCH):
                        p.do(T, lambda hc=hc, PP=PP, hh=hh: T.stream.matmul(
                            PP[:, :],
                            hqT[:, hc * GSZ * 128 + i * 128: hc * GSZ * 128 + (i + 1) * 128],
                            wdq_s[:, hc * D + hh * 512: hc * D + (hh + 1) * 512],
                            start=(hc == 0), stop=(hc == NCH - 1)),
                            rd=["hqT%d" % i, "wdq_s"], wr=[pk], sync=(hc == NCH - 1))
                p.do(V, lambda: V.stream.tensor_add(
                    oa[:, 0:512], P0[:, :], x2g[:, i * D:i * D + 512]),
                    rd=[k0, "x2g%d" % i], wr=[ok])
                p.do(V, lambda: V.stream.tensor_add(
                    oa[:, 512:1024], P1[:, :], x2g[:, i * D + 512:(i + 1) * D]),
                    rd=[k1, "x2g%d" % i], wr=[ok])
                p.dma(DSY, out_d[s, qt * 128:(qt + 1) * 128, :], oa[:],
                      rd=[ok], wr=["d:out"])
                return p

            for i in range(0, GSZ, 2):
                emit(ser, dn_plan(i, 0), dn_plan(i + 1, 1))

    ser.finish()
    stack.close()
    return nc



def kernel(L0_attn_x, L0_attn_Wqk, L0_attn_bqk, L0_attn_gamma, L0_attn_beta,
           L0_attn_Wv, L0_attn_bv, L0_attn_rel_emb, L0_ffn_w_gate,
           L0_ffn_w_up, L0_ffn_w_down):
    from concourse.bass_utils import run_bass_kernel_spmd
    x = np.ascontiguousarray(np.asarray(L0_attn_x, dtype=np.float32))
    B, S_, D_ = x.shape
    n_cores = 8
    nseq = B // n_cores
    nc = bass.Bass("TRN2", target_bir_lowering=False)
    build(nc, S=S_, NSEQ=nseq)
    common = {
        "wqk": np.ascontiguousarray(L0_attn_Wqk, dtype=np.float32),
        "bqk": np.ascontiguousarray(np.asarray(L0_attn_bqk, dtype=np.float32).reshape(E, 1)),
        "gam": np.ascontiguousarray(L0_attn_gamma, dtype=np.float32),
        "bet": np.ascontiguousarray(L0_attn_beta, dtype=np.float32),
        "wv": np.ascontiguousarray(L0_attn_Wv, dtype=np.float32),
        "bv": np.ascontiguousarray(np.asarray(L0_attn_bv, dtype=np.float32).reshape(1, D)),
        "rel": np.ascontiguousarray(L0_attn_rel_emb, dtype=np.float32),
        "wg": np.ascontiguousarray(L0_ffn_w_gate, dtype=np.float32),
        "wu": np.ascontiguousarray(L0_ffn_w_up, dtype=np.float32),
        "wd": np.ascontiguousarray(L0_ffn_w_down, dtype=np.float32),
    }
    in_maps = [dict(common, x=np.ascontiguousarray(x[c * nseq:(c + 1) * nseq]))
               for c in range(n_cores)]
    res = run_bass_kernel_spmd(nc, in_maps, core_ids=list(range(n_cores)))
    return np.concatenate([np.asarray(r["out"]) for r in res.results], axis=0)
